# version 12
# baseline (speedup 1.0000x reference)
import contextlib
import numpy as np
import concourse.bass as bass
import concourse.mybir as mybir
from concourse.bass_utils import run_bass_kernel_spmd

F32 = mybir.dt.float32
BF16 = mybir.dt.bfloat16
I32 = mybir.dt.int32
AF = mybir.ActivationFunctionType
ALU = mybir.AluOpType
AX = mybir.AxisListType

B, T, D, NST, R = 4, 4096, 1024, 16, 64
E, I_FF = 8, 2816
NH, DH = 16, 64
TPC = 2048
DC = D // 128
LN_EPS = 1e-5
RG = [[0, 1], [2, 3], [4, 5], [6, 7]]

N_SUBBLOCKS = 4
DEBUG_TAPS = False
PARANOID_SYNC = False
FULL_SYNC = False


class Op:
    __slots__ = ("eng", "fn", "kind", "deps", "inc", "sem_key", "num", "name",
                 "small", "snapshot")

    def __init__(self, eng, fn, kind, sem_key, name, small=False):
        self.eng, self.fn, self.kind = eng, fn, kind
        self.sem_key, self.name = sem_key, name
        self.deps = []
        self.inc = False
        self.num = None
        self.small = small
        self.snapshot = None


class Prog:
    ENGS = ("pe", "act", "dve", "pool", "sp")
    BLOCK_ATTR = {"pe": "tensor", "act": "scalar", "dve": "vector",
                  "pool": "gpsimd", "sp": "sync"}

    def __init__(self, nc):
        self.nc = nc
        self.ops = {e: [] for e in self.ENGS}
        self.last_w = {}
        self.readers = {}
        self.sems = {}
        self.sem_keys = set()
        self.dma_counts = {}
        self.eng_dma = {e: {} for e in self.ENGS}
        self.cc_idx = 0
        self.n_barriers = 0
        self.regs = {}

    def add(self, eng, fn, r=(), w=(), kind="cmp", name="", sem_key=None, small=False):
        if kind == "dma":
            if sem_key is None:
                sem_key = ("b", id(w[0])) if w else (eng, "d")
        elif kind == "cc":
            self.cc_idx += 1
            sem_key = ("cc", self.cc_idx)
        elif sem_key is None:
            sem_key = (eng, "c")
        op = Op(eng, fn, kind, sem_key, name, small=small)
        if kind == "dma":
            self.dma_counts[sem_key] = self.dma_counts.get(sem_key, 0) + 1
            op.num = 16 * self.dma_counts[sem_key]
            self.eng_dma[eng][sem_key] = op.num
            op.inc = True
        elif kind == "cc":
            op.num = 1
            self.eng_dma[eng][sem_key] = op.num
            op.inc = True

        def want_edge(prod):
            if prod.eng != eng:
                return True
            if prod.kind == "cmp" and kind == "cmp":
                if eng == "pe":
                    return False
                return prod.small or FULL_SYNC
            if prod.kind == "dma" and kind == "dma" and prod.sem_key == sem_key:
                return False
            if prod.kind == "cmp" and kind in ("dma", "cc"):
                return False
            return True

        def dep_of(prod):
            if prod.kind in ("dma", "cc"):
                if prod.kind == "dma":
                    return ("fixed", prod.sem_key, 16 * self.dma_counts[prod.sem_key])
                return ("fixed", prod.sem_key, prod.num)
            return ("op", prod)

        added = set()
        def add_dep(prod):
            if prod is op or id(prod) in added:
                return
            if want_edge(prod):
                op.deps.append(dep_of(prod))
                added.add(id(prod))
                if prod.kind == "cmp":
                    prod.inc = True
        for t in r:
            lw = self.last_w.get(id(t))
            if lw is not None:
                add_dep(lw)
        for t in w:
            lw = self.last_w.get(id(t))
            if lw is not None:
                add_dep(lw)
            for rd in self.readers.get(id(t), ()):
                add_dep(rd)
        for t in w:
            self.last_w[id(t)] = op
            self.readers[id(t)] = []
        for t in r:
            self.readers.setdefault(id(t), []).append(op)
        self.ops[eng].append(op)
        self.sem_keys.add(sem_key)
        return op

    def raw(self, eng, fn, name="raw"):
        op = Op(eng, fn, "raw", None, name)
        self.ops[eng].append(op)
        return op

    def barrier(self):
        self.n_barriers += 1
        self.sem_keys.add(("bar", "c"))
        nb = self.n_barriers
        for eng in self.ENGS:
            op = Op(eng, None, "barsig", ("bar", "c"), f"barsig{nb}")
            op.inc = True
            op.snapshot = dict(self.eng_dma[eng])
            self.ops[eng].append(op)
        for eng in self.ENGS:
            self.ops[eng].append(Op(eng, None, "barwait", None, f"barwait{nb}"))
        self.last_w.clear()
        self.readers.clear()

    def finalize_numbering(self):
        counts = {k: 0 for k in self.sem_keys}
        for k, v in self.dma_counts.items():
            counts[k] = 16 * v
        counts[("cc", "n")] = self.cc_idx
        for eng in self.ENGS:
            for op in self.ops[eng]:
                if op.kind == "barsig":
                    counts[op.sem_key] += 1
                    op.num = counts[op.sem_key]
                elif op.kind == "cmp" and op.inc:
                    counts[op.sem_key] = counts.get(op.sem_key, 0) + 1
                    op.num = counts[op.sem_key]
        self.final_counts = counts

    def emit(self, block, sems):
        self.sems = sems
        for eng in self.ENGS:
            def body(e, eng=eng):
                watermark = {}
                for op in self.ops[eng]:
                    if op.kind == "barsig":
                        for sk, tot in op.snapshot.items():
                            if watermark.get(sk, 0) < tot:
                                e.wait_ge(sems[sk], tot)
                                watermark[sk] = tot
                        e.nop().then_inc(sems[("bar", "c")], 1)
                        continue
                    if op.kind == "barwait":
                        e.wait_ge(sems[("bar", "c")], 5 * int(op.name[7:]))
                        continue
                    if op.kind == "raw":
                        op.fn(e, self)
                        continue
                    for d in op.deps:
                        if d[0] == "fixed":
                            sk, v = d[1], d[2]
                        else:
                            sk, v = d[1].sem_key, d[1].num
                            if v is None:
                                continue
                        if watermark.get(sk, 0) < v:
                            e.wait_ge(sems[sk], v)
                            watermark[sk] = v
                    inst = op.fn(e)
                    if op.inc:
                        inst.then_inc(sems[op.sem_key],
                                      16 if op.kind == "dma" else 1)
            getattr(block, self.BLOCK_ATTR[eng])(body)


def build_program(n_sub):
    nc = bass.Bass("TRN2", target_bir_lowering=False)

    xw = nc.dram_tensor("xw", [D, TPC + 2], F32, kind="ExternalInput")
    WinT = nc.dram_tensor("WinT", [D, 2 * D], F32, kind="ExternalInput")
    convw = nc.dram_tensor("convw", [D, 3], F32, kind="ExternalInput")
    WxT = nc.dram_tensor("WxT", [D, R + 2 * NST], F32, kind="ExternalInput")
    WdtT = nc.dram_tensor("WdtT", [R, D], F32, kind="ExternalInput")
    bdt = nc.dram_tensor("bdt", [D, 1], F32, kind="ExternalInput")
    Alog = nc.dram_tensor("Alog", [D, NST], F32, kind="ExternalInput")
    Dprm = nc.dram_tensor("Dprm", [D, 1], F32, kind="ExternalInput")
    WCT = nc.dram_tensor("WCT", [NST, D], F32, kind="ExternalInput")
    WoutT = nc.dram_tensor("WoutT", [D, D], F32, kind="ExternalInput")
    W1 = nc.dram_tensor("W1", [D, 4 * D], F32, kind="ExternalInput")
    b1 = nc.dram_tensor("b1", [4 * D, 1], F32, kind="ExternalInput")
    W2 = nc.dram_tensor("W2", [D, D], F32, kind="ExternalInput")
    b2 = nc.dram_tensor("b2", [D, 1], F32, kind="ExternalInput")
    gamma = nc.dram_tensor("gamma", [D, 1], F32, kind="ExternalInput")
    beta = nc.dram_tensor("beta", [D, 1], F32, kind="ExternalInput")
    rwT = nc.dram_tensor("rwT", [D, E], F32, kind="ExternalInput")
    gwT = nc.dram_tensor("gwT", [E, D, I_FF], F32, kind="ExternalInput")
    uwT = nc.dram_tensor("uwT", [E, D, I_FF], F32, kind="ExternalInput")
    dwT = nc.dram_tensor("dwT", [E, I_FF, D], F32, kind="ExternalInput")
    cfg = nc.dram_tensor("cfg", [1, 8], I32, kind="ExternalInput")
    eyeI = nc.dram_tensor("eyeI", [128, 128], F32, kind="ExternalInput")
    selI = nc.dram_tensor("selI", [NST, NST * 128], F32, kind="ExternalInput")

    out = nc.dram_tensor("out", [D, TPC], F32, kind="ExternalOutput")

    XR = nc.dram_tensor("XR", [D, TPC], F32)
    SILZD = nc.dram_tensor("SILZD", [D, TPC], F32)
    UD = nc.dram_tensor("UD", [D, TPC], F32)
    CCCI = nc.dram_tensor("CCCI", [D, NST], F32)
    CCCO = nc.dram_tensor("CCCO", [2 * D, NST], F32)
    CCXI = nc.dram_tensor("CCXI", [D, TPC], BF16)
    CCXO = nc.dram_tensor("CCXO", [2 * D, TPC], BF16)
    GW16 = nc.dram_tensor("GW16", [E, D, I_FF], BF16)
    WO16 = nc.dram_tensor("WO16", [D, D], BF16)
    UW16 = nc.dram_tensor("UW16", [E, D, I_FF], BF16)
    DW16 = nc.dram_tensor("DW16", [E, I_FF, D], BF16)
    if DEBUG_TAPS:
        DBG = {nm: nc.dram_tensor(nm, [D, TPC], F32, kind="ExternalOutput")
               for nm in ("dbg_xn", "dbg_xs", "dbg_y")}
        DBG["dbg_xd"] = nc.dram_tensor("dbg_xd", [96, TPC], F32, kind="ExternalOutput")
        DBG["dbg_ss"] = nc.dram_tensor("dbg_ss", [128, TPC], F32, kind="ExternalOutput")
        DBG["dbg_dt"] = nc.dram_tensor("dbg_dt", [128, TPC], F32, kind="ExternalOutput")
        DBG["dbg_xnh"] = nc.dram_tensor("dbg_xnh", [D, 2], F32, kind="ExternalOutput")
        DBG["dbg_rows"] = nc.dram_tensor("dbg_rows", [1, 512 * 3], F32, kind="ExternalOutput")
        DBG["dbg_xc"] = nc.dram_tensor("dbg_xc", [128, TPC + 2], F32, kind="ExternalOutput")

    prog = Prog(nc)

    PS = [nc.alloc_psum_tensor(f"ps{i}", [128, 512], F32) for i in range(8)]
    XN = [nc.alloc_sbuf_tensor(f"xn{i}", [128, TPC], BF16) for i in range(DC)]
    XNH = [nc.alloc_sbuf_tensor(f"xnh{i}", [128, 2], BF16) for i in range(DC)]
    XA = [nc.alloc_sbuf_tensor(f"xa{i}", [128, TPC], F32) for i in range(DC)]
    SB5 = [nc.alloc_sbuf_tensor(f"sb5{i}", [128, 512], F32) for i in range(4)]
    ROWS = nc.alloc_sbuf_tensor("rows", [1, 512 * 3], F32)
    ONESC = nc.alloc_sbuf_tensor("onesc", [128, 2], BF16)
    ONESR = nc.alloc_sbuf_tensor("onesr", [1, 128], F32)
    ONESRB = nc.alloc_sbuf_tensor("onesrb", [1, 128], BF16)
    CFGS = nc.alloc_sbuf_tensor("cfgs", [1, 8], I32)
    EYE = nc.alloc_sbuf_tensor("eye", [128, 128], F32)
    CONSTS = nc.alloc_sbuf_tensor("consts", [128, 2], F32)
    nc.const_aps.register(F32, 1e-5, CONSTS[:, 0:1])

    def dma(eng, outap, inap, r=(), w=(), name="dma", sem_key=None, slow=False):
        return prog.add(eng, lambda e, o=outap, i=inap, s=slow: e.dma_start(
            out=o, in_=i, allow_slow_non_contiguous=s),
            r=r, w=w, kind="dma", name=name, sem_key=sem_key)

    def cmp(eng, fn, r=(), w=(), name="", small=False, sem_key=None):
        return prog.add(eng, fn, r=r, w=w, kind="cmp", name=name, small=small,
                        sem_key=sem_key)

    def rowslc(j, w=512):
        return ROWS[0:1, j * 512:j * 512 + w]

    cfg_ld = dma("sp", CFGS[:, :], cfg[:, :], w=[CFGS], name="cfgld",
                 sem_key=("su0", "d"))
    cmp("dve", lambda e: e.memset(ONESC[:, :], 1.0), w=[ONESC])
    cmp("dve", lambda e: e.memset(CONSTS[:, 0:1], 1e-5), w=[CONSTS])
    cmp("dve", lambda e: e.memset(CONSTS[:, 1:2], 1.0 / T), w=[CONSTS])
    cmp("dve", lambda e: e.memset(ONESR[:, :], 1.0), w=[ONESR])
    cmp("dve", lambda e: e.memset(ONESRB[:, :], 1.0), w=[ONESRB])

    def get_parity(e, prog):
        key = (str(e.engine), "parity")
        if key not in prog.regs:
            reg = e.alloc_register("parity")
            e.wait_ge(prog.sems[("su0", "d")], 16 * prog.dma_counts[("su0", "d")])
            e.reg_load(reg, CFGS[0:1, 0:1])
            prog.regs[key] = reg
        return prog.regs[key]

    def get_reg(e, prog, name):
        key = (str(e.engine), name)
        if key not in prog.regs:
            prog.regs[key] = e.alloc_register(name)
        return prog.regs[key]

    def emit_ln(src, srcT, gb, tag):
        with contextlib.ExitStack() as st:
            XBB = [st.enter_context(nc.sbuf_tensor(f"{tag}xb{i}", [128, 512], BF16))
                   for i in range(2)]
            SQB = [st.enter_context(nc.sbuf_tensor(f"{tag}sq{i}", [128, 512], BF16))
                   for i in range(2)]
            for tt in range(4):
                sl = slice(tt * 512, (tt + 1) * 512)
                for dc in range(DC):
                    xb, sq = XBB[dc % 2], SQB[dc % 2]
                    cmp("act", lambda e, dc=dc, sl=sl, xb=xb: e.activation(
                        out=xb[:, 0:512], in_=src(dc)[:, sl], func=AF.Copy),
                        r=[srcT(dc)], w=[xb], name=f"{tag}cp")
                    cmp("act", lambda e, dc=dc, sl=sl, sq=sq: e.activation(
                        out=sq[:, 0:512], in_=src(dc)[:, sl], func=AF.Square),
                        r=[srcT(dc)], w=[sq], name=f"{tag}sq")
                    cmp("pe", lambda e, dc=dc, xb=xb: e.matmul(
                        PS[6][0:1, 0:512], ONESC[:, 0:1], xb[:, 0:512],
                        start=(dc == 0), stop=(dc == DC - 1)),
                        r=[xb, ONESC], w=[PS[6]], name=f"{tag}s1")
                    cmp("pe", lambda e, dc=dc, sq=sq: e.matmul(
                        PS[7][0:1, 0:512], ONESC[:, 0:1], sq[:, 0:512],
                        start=(dc == 0), stop=(dc == DC - 1)),
                        r=[sq, ONESC], w=[PS[7]], name=f"{tag}s2")
                ln_rows_and_bcast(512)
                for dc in range(DC):
                    tmp = SB5[dc % 2]
                    cmp("dve", lambda e, dc=dc, sl=sl, tmp=tmp: e.tensor_tensor(
                        tmp[:, 0:512], src(dc)[:, sl], PS[4][:, 0:512], ALU.mult),
                        r=[srcT(dc), PS[4]], w=[tmp], name=f"{tag}n1")
                    if not gb:
                        cmp("dve", lambda e, dc=dc, sl=sl, tmp=tmp: e.tensor_sub(
                            XN[dc][:, sl], tmp[:, 0:512], PS[5][:, 0:512]),
                            r=[tmp, PS[5]], w=[XN[dc]], name=f"{tag}n2")
                    else:
                        gams, bets = gb
                        cmp("dve", lambda e, dc=dc, sl=sl, tmp=tmp: e.tensor_sub(
                            tmp[:, 0:512], tmp[:, 0:512], PS[5][:, 0:512]),
                            r=[tmp, PS[5]], w=[tmp], name=f"{tag}n2")
                        cmp("dve", lambda e, dc=dc, sl=sl, tmp=tmp, gams=gams, bets=bets: e.tensor_scalar(
                            XN[dc][:, sl], tmp[:, 0:512],
                            gams[:, dc:dc + 1], bets[:, dc:dc + 1],
                            ALU.mult, ALU.add),
                            r=[tmp, gams, bets], w=[XN[dc]], name=f"{tag}n3")

    def ln_rows_and_bcast(w):
        cmp("dve", lambda e, w=w: e.tensor_scalar_mul(
            rowslc(0, w), PS[6][0:1, 0:w], 1.0 / D), r=[PS[6]], w=[ROWS], small=True)
        cmp("dve", lambda e, w=w: e.tensor_scalar_mul(
            rowslc(1, w), PS[7][0:1, 0:w], 1.0 / D), r=[PS[7]], w=[ROWS], small=True)
        cmp("dve", lambda e, w=w: e.tensor_tensor(
            rowslc(2, w), rowslc(0, w), rowslc(0, w), ALU.mult), r=[ROWS], w=[ROWS], small=True)
        cmp("dve", lambda e, w=w: e.tensor_sub(
            rowslc(1, w), rowslc(1, w), rowslc(2, w)), r=[ROWS], w=[ROWS], small=True)
        cmp("act", lambda e, w=w: e.activation(
            out=rowslc(2, w), in_=rowslc(1, w), func=AF.Ln, bias=LN_EPS),
            r=[ROWS, CONSTS], w=[ROWS], small=True)
        cmp("act", lambda e, w=w: e.activation(
            out=rowslc(2, w), in_=rowslc(2, w), func=AF.Exp, scale=-0.5),
            r=[ROWS], w=[ROWS], small=True)
        cmp("dve", lambda e, w=w: e.tensor_tensor(
            rowslc(1, w), rowslc(0, w), rowslc(2, w), ALU.mult), r=[ROWS], w=[ROWS], small=True)
        cmp("pe", lambda e, w=w: e.matmul(
            PS[4][:, 0:w], ONESR[0:1, :], rowslc(2, w), start=True, stop=True),
            r=[ROWS, ONESR], w=[PS[4]])
        cmp("pe", lambda e, w=w: e.matmul(
            PS[5][:, 0:w], ONESR[0:1, :], rowslc(1, w), start=True, stop=True),
            r=[ROWS, ONESR], w=[PS[5]])

    def phase1(dst, is_out):
        st = contextlib.ExitStack()
        sb = lambda nm, shp, dt=F32: st.enter_context(nc.sbuf_tensor(nm, shp, dt))
        WI = [sb(f"p1wi{i}", [128, 8 * 128], BF16) for i in range(2)]
        XBB = [sb(f"p1xb{i}", [128, 512], BF16) for i in range(2)]
        SQB = [sb(f"p1sq{i}", [128, 512], BF16) for i in range(2)]
        XC = sb("p1xc", [128, TPC + 2])
        SCA = [sb("p1a0", [128, TPC])]
        SCB = sb("p1bx", [128, TPC])
        SCS = sb("p1s", [128, TPC])
        SSUM = sb("p1ss", [128, TPC])
        DX = sb("p1dx", [128, TPC])
        DT = sb("p1dt", [128, TPC])
        XDBL = sb("p1xd", [96, TPC])
        XDBC = sb("p1xdc", [80, TPC], BF16)
        WO = [sb("p1wo", [128, 1024], BF16)]
        SELS = sb("sels", [NST, NST * 128])
        dma("sp", SELS[:, :], selI[:, :], w=[SELS], sem_key=("su1", "d"))
        CONVS = sb("convs", [128, 3 * DC])
        BDTS = sb("bdts", [128, DC])
        NLAM = sb("nlam", [128, NST * DC])
        DPRS = sb("dprs", [128, DC])
        WXTS = sb("wxts", [128, (R + 2 * NST) * DC], BF16)
        WDTS = sb("wdts", [R, D], BF16)
        WCTS = sb("wcts", [80, D], BF16)
        CAR1 = [sb(f"car1_{i}", [128, NST]) for i in range(DC)]
        CAR2 = [sb(f"car2_{i}", [128, NST]) for i in range(DC)]
        for i in range(DC):
            car1_ms = cmp("dve", lambda e, i=i: e.memset(CAR1[i][:, :], 0.0), w=[CAR1[i]])
            car1_ms.inc = True
            cmp("dve", lambda e, i=i: e.memset(CAR2[i][:, :], 0.0), w=[CAR2[i]])
        dma("sp", CONVS.ap().rearrange("p (a k) -> p a k", a=DC), convw.ap().rearrange("(a p) k -> p a k", p=128), w=[CONVS], sem_key=("su1", "d"))
        dma("sp", BDTS.ap().rearrange("p (a k) -> p a k", a=DC), bdt.ap().rearrange("(a p) k -> p a k", p=128), w=[BDTS], slow=True, sem_key=("su1", "d"))
        dma("sp", NLAM.ap().rearrange("p (a n) -> p a n", a=DC), Alog.ap().rearrange("(a p) n -> p a n", p=128), w=[NLAM], sem_key=("su1", "d"))
        dma("sp", DPRS.ap().rearrange("p (a k) -> p a k", a=DC), Dprm.ap().rearrange("(a p) k -> p a k", p=128), w=[DPRS], slow=True, sem_key=("su1", "d"))
        dma("pool", WXTS.ap().rearrange("p (a j) -> p a j", a=DC), WxT.ap().rearrange("(a p) j -> p a j", p=128), w=[WXTS], sem_key=("sp1", "d"))
        dma("pool", WDTS[:, :], WdtT[:, :], w=[WDTS], sem_key=("sp1", "d"))
        dma("pool", WCTS[64:80, :], WCT[:, :], w=[WCTS], sem_key=("sp1", "d"))
        cmp("act", lambda e: e.activation(out=NLAM[:, :], in_=NLAM[:, :], func=AF.Exp),
            r=[NLAM], w=[NLAM])
        cmp("dve", lambda e: e.tensor_scalar_mul(NLAM[:, :], NLAM[:, :], -1.0),
            r=[NLAM], w=[NLAM])
        dma("pool", WO16[:, :], WoutT[:, :], w=[WO16], name="wcvo")

        for tt in range(5):
            w = 512 if tt < 4 else 2
            for phase in ("stats", "norm"):
                for dc in range(DC):
                    xt = SB5[2 + dc % 2]
                    if tt < 4:
                        srcp = xw[dc * 128:(dc + 1) * 128, 1 + tt * 512:1 + (tt + 1) * 512]
                        dma("sp", xt[:, 0:512], srcp, w=[xt], name="p1lnld")
                    else:
                        dma("sp", xt[:, 0:1], xw[dc * 128:(dc + 1) * 128, 0:1],
                            w=[xt], name="p1lnlh0", slow=True)
                        dma("sp", xt[:, 1:2], xw[dc * 128:(dc + 1) * 128, TPC + 1:TPC + 2],
                            w=[xt], name="p1lnlh1", slow=True)
                    if phase == "stats":
                        xb, sq = XBB[dc % 2], SQB[dc % 2]
                        cmp("act", lambda e, w=w, xt=xt, xb=xb: e.activation(
                            out=xb[:, 0:w], in_=xt[:, 0:w], func=AF.Copy),
                            r=[xt], w=[xb])
                        cmp("act", lambda e, w=w, xt=xt, sq=sq: e.activation(
                            out=sq[:, 0:w], in_=xt[:, 0:w], func=AF.Square),
                            r=[xt], w=[sq])
                        cmp("pe", lambda e, dc=dc, w=w, xb=xb: e.matmul(
                            PS[6][0:1, 0:w], ONESC[:, 0:1], xb[:, 0:w],
                            start=(dc == 0), stop=(dc == DC - 1)),
                            r=[xb, ONESC], w=[PS[6]])
                        cmp("pe", lambda e, dc=dc, w=w, sq=sq: e.matmul(
                            PS[7][0:1, 0:w], ONESC[:, 0:1], sq[:, 0:w],
                            start=(dc == 0), stop=(dc == DC - 1)),
                            r=[sq, ONESC], w=[PS[7]])
                    else:
                        tmp = SB5[dc % 2]
                        dstt = XN[dc] if tt < 4 else XNH[dc]
                        dap = XN[dc][:, tt * 512:(tt + 1) * 512] if tt < 4 else XNH[dc][:, 0:2]
                        cmp("dve", lambda e, w=w, xt=xt, tmp=tmp: e.tensor_tensor(
                            tmp[:, 0:w], xt[:, 0:w], PS[4][:, 0:w], ALU.mult),
                            r=[xt, PS[4]], w=[tmp], small=(w == 2))
                        cmp("dve", lambda e, w=w, tmp=tmp, dap=dap, dstt=dstt: e.tensor_sub(
                            dap, tmp[:, 0:w], PS[5][:, 0:w]),
                            r=[tmp, PS[5]], w=[dstt], small=(w == 2))
                if phase == "stats":
                    if DEBUG_TAPS and tt == 4:
                        cmp("dve", lambda e: e.tensor_copy(ROWS[0:1, 4:6], PS[6][0:1, 0:2]),
                            r=[PS[6]], w=[ROWS])
                        cmp("dve", lambda e: e.tensor_copy(ROWS[0:1, 6:8], PS[7][0:1, 0:2]),
                            r=[PS[7]], w=[ROWS])
                    ln_rows_and_bcast(w)
                    if DEBUG_TAPS and tt == 4:
                        dma("sp", DBG["dbg_rows"][:, :], ROWS[:, :], r=[ROWS], name="dbgrows")

        if DEBUG_TAPS:
            for dc in range(DC):
                dma("pool", DBG["dbg_xn"][dc * 128:(dc + 1) * 128, :], XN[dc][:, :],
                    r=[XN[dc]], name="dbgxn")
                dma("pool", DBG["dbg_xnh"][dc * 128:(dc + 1) * 128, :], XNH[dc][:, :],
                    r=[XNH[dc]], name="dbgxnh")
        for ec in range(16):
            wi = WI[ec % 2]
            dma("pool", wi[:, :],
                WinT.ap().rearrange("(a p) ee -> p a ee", p=128)[:, :, ec * 128:(ec + 1) * 128],
                w=[wi], name="p1wild")
            for tt in range(4):
                psA = PS[tt % 2]
                for dc in range(DC):
                    cmp("pe", lambda e, tt=tt, dc=dc, wi=wi, psA=psA: e.matmul(
                        psA[:, 0:512], wi[:, dc * 128:(dc + 1) * 128],
                        XN[dc][:, tt * 512:(tt + 1) * 512],
                        start=(dc == 0), stop=(dc == DC - 1)),
                        r=[wi, XN[dc]], w=[psA], name="p1xz")
                if ec < 8:
                    cmp("dve", lambda e, tt=tt, psA=psA: e.tensor_copy(
                        XC[:, 1 + tt * 512:1 + (tt + 1) * 512], psA[:, 0:512]),
                        r=[psA], w=[XC], name="p1xcev")
                else:
                    sz = SB5[2 + tt % 2]
                    cmp("act", lambda e, sz=sz, psA=psA: e.activation(
                        out=sz[:, 0:512], in_=psA[:, 0:512], func=AF.Silu),
                        r=[psA], w=[sz], name="p1silz")
                    dma("sp", SILZD[(ec - 8) * 128:(ec - 7) * 128, tt * 512:(tt + 1) * 512],
                        sz[:, 0:512], r=[sz], w=[SILZD], name="p1szst")
            psB = PS[2]
            for dc in range(DC):
                cmp("pe", lambda e, dc=dc, wi=wi, psB=psB: e.matmul(
                    psB[:, 0:2], wi[:, dc * 128:(dc + 1) * 128], XNH[dc][:, 0:2],
                    start=(dc == 0), stop=(dc == DC - 1)),
                    r=[wi, XNH[dc]], w=[psB], name="p1xzh")
            if ec < 8:
                cmp("dve", lambda e, psB=psB: e.tensor_copy(
                    XC[:, 0:1], psB[:, 0:1]), r=[psB], w=[XC], small=True)
                cmp("dve", lambda e, psB=psB: e.tensor_copy(
                    XC[:, TPC + 1:TPC + 2], psB[:, 1:2]), r=[psB], w=[XC], small=True)
                w0 = CONVS[:, ec * 3 + 0:ec * 3 + 1]
                w1 = CONVS[:, ec * 3 + 1:ec * 3 + 2]
                w2 = CONVS[:, ec * 3 + 2:ec * 3 + 3]
                cmp("dve", lambda e, w1=w1: e.tensor_scalar(
                    DX[:, :], XC[:, 1:TPC + 1], w1, None, ALU.mult),
                    r=[XC, CONVS], w=[DX], name="p1cv1")
                cmp("dve", lambda e, w0=w0: e.scalar_tensor_tensor(
                    DX[:, :], XC[:, 0:TPC], w0, DX[:, :], ALU.mult, ALU.add),
                    r=[XC, CONVS, DX], w=[DX], name="p1cv2")
                cmp("dve", lambda e, w2=w2: e.scalar_tensor_tensor(
                    DX[:, :], XC[:, 2:TPC + 2], w2, DX[:, :], ALU.mult, ALU.add),
                    r=[XC, CONVS, DX], w=[DX], name="p1cv3")
                if DEBUG_TAPS and ec == 0:
                    dma("sp", DBG["dbg_xc"][:, :], XC[:, :], r=[XC], name="dbgxc")
                cmp("act", lambda e, ec=ec: e.activation(
                    out=XA[ec][:, :], in_=DX[:, :], func=AF.Silu),
                    r=[DX], w=[XA[ec]], name="p1silxs")

        if DEBUG_TAPS:
            for dc in range(DC):
                dma("sp", DBG["dbg_xs"][dc * 128:(dc + 1) * 128, :], XA[dc][:, :],
                    r=[XA[dc]], name="dbgxs")
        for tt in range(4):
            ps = PS[4 + tt % 2]
            for dc in range(DC):
                xb = XBB[dc % 2]
                cmp("dve", lambda e, tt=tt, dc=dc, xb=xb: e.tensor_copy(
                    xb[:, 0:512], XA[dc][:, tt * 512:(tt + 1) * 512]),
                    r=[XA[dc]], w=[xb], name="p1xsbf")
                cmp("pe", lambda e, tt=tt, dc=dc, ps=ps, xb=xb: e.matmul(
                    ps[0:96, 0:512], WXTS[:, dc * 96:(dc + 1) * 96], xb[:, 0:512],
                    start=(dc == 0), stop=(dc == DC - 1)),
                    r=[WXTS, xb], w=[ps], name="p1xdbl")
            cmp("dve", lambda e, tt=tt, ps=ps: e.tensor_copy(
                XDBL[:, tt * 512:(tt + 1) * 512], ps[0:96, 0:512]),
                r=[ps], w=[XDBL], name="p1xdev")

        dma("pool", XDBC[0:64, :], XDBL[32:96, :], r=[XDBL], w=[XDBC], name="p1dtb")
        dma("pool", XDBC[64:80, :], XDBL[16:32, :], r=[XDBL], w=[XDBC], name="p1cpb")

        if DEBUG_TAPS:
            dma("sp", DBG["dbg_xd"][:, :], XDBL[:, :], r=[XDBL], name="dbgxd")
        for dc in range(DC):
            for tt in range(4):
                ps = PS[4 + tt % 2]
                cmp("pe", lambda e, dc=dc, tt=tt, ps=ps: e.matmul(
                    ps[:, 0:512], WDTS[:, dc * 128:(dc + 1) * 128],
                    XDBC[0:64, tt * 512:(tt + 1) * 512], start=True, stop=True),
                    r=[WDTS, XDBC], w=[ps], name="p1dtmm")
                bd = BDTS[:, dc:dc + 1]
                ab, rl = SB5[0], SB5[1]
                cmp("act", lambda e, ps=ps, bd=bd, ab=ab: e.activation(
                    out=ab[:, 0:512], in_=ps[:, 0:512], func=AF.Abs, bias=bd),
                    r=[ps, BDTS], w=[ab], name="p1abs")
                cmp("act", lambda e, ps=ps, bd=bd, rl=rl: e.activation(
                    out=rl[:, 0:512], in_=ps[:, 0:512], func=AF.Relu, bias=bd),
                    r=[ps, BDTS], w=[rl], name="p1relu")
                cmp("act", lambda e, ab=ab: e.activation(
                    out=ab[:, 0:512], in_=ab[:, 0:512], func=AF.Exp, scale=-1.0),
                    r=[ab], w=[ab], name="p1exp")
                cmp("act", lambda e, ab=ab: e.activation(
                    out=ab[:, 0:512], in_=ab[:, 0:512], func=AF.Ln, bias=1.0),
                    r=[ab], w=[ab], name="p1ln1p")
                cmp("dve", lambda e, tt=tt, ab=ab, rl=rl: e.scalar_tensor_tensor(
                    DT[:, tt * 512:(tt + 1) * 512], ab[:, 0:512], CONSTS[:, 0:1],
                    rl[:, 0:512], ALU.add, ALU.add),
                    r=[ab, rl, CONSTS], w=[DT], name="p1dtc")
            cmp("dve", lambda e, dc=dc: e.tensor_mul(DX[:, :], DT[:, :], XA[dc][:, :]),
                r=[DT, XA[dc]], w=[DX], name="p1dxs")

            def carry_wait(e, prog, dc=dc):
                preg = get_parity(e, prog)
                t = get_reg(e, prog, "cwt")
                e.reg_mul(t, preg, 16)
                e.wait_ge(prog.sems[("carry", dc)], t)
            prog.raw("dve", carry_wait, name="p1cwait")

            for n in range(NST):
                sca_t = SCA[0] if n % 2 == 0 else XC
                sca = sca_t[:, 0:TPC]
                for tt in range(4):
                    ps = PS[tt % 2]
                    cmp("pe", lambda e, n=n, tt=tt, ps=ps: e.matmul(
                        ps[:, 0:512], SELS[:, n * 128:(n + 1) * 128],
                        XDBL[0:16, tt * 512:(tt + 1) * 512], start=True, stop=True),
                        r=[SELS, XDBL], w=[ps], name="p1bb")
                    cmp("dve", lambda e, tt=tt, ps=ps: e.tensor_mul(
                        SCB[:, tt * 512:(tt + 1) * 512], DX[:, tt * 512:(tt + 1) * 512],
                        ps[:, 0:512]),
                        r=[DX, ps], w=[SCB], name="p1bx")
                cmp("act", lambda e, dc=dc, n=n, sca=sca: e.activation(
                    out=sca, in_=DT[:, :], func=AF.Exp,
                    scale=NLAM[:, dc * NST + n:dc * NST + n + 1]),
                    r=[DT, NLAM], w=[sca_t], name="p1adisc")
                scop = prog.add("dve", lambda e, dc=dc, n=n, sca=sca: e.tensor_tensor_scan(
                    SCS[:, :], sca, SCB[:, :], CAR2[dc][:, n:n + 1],
                    ALU.mult, ALU.add),
                    r=[sca_t, SCB], w=[SCS], kind="cmp", name="p1scan")
                scop.inc = True
                if n == 0:
                    ssop = cmp("dve", lambda e: e.tensor_copy(SSUM[:, :], SCS[:, :]),
                               r=[SCS], w=[SSUM], name="p1ss0")
                else:
                    ssop = cmp("dve", lambda e: e.tensor_add(SSUM[:, :], SSUM[:, :], SCS[:, :]),
                               r=[SCS, SSUM], w=[SSUM], name="p1ssa")
                ssop.inc = True

                def car_copy(e, prog, dc=dc, n=n, scop=scop):
                    preg = get_parity(e, prog)
                    e.wait_ge(prog.sems[("dve", "c")], scop.num)
                    with e.If_eq(preg, 0):
                        e.tensor_copy(CAR1[dc][:, n:n + 1],
                                      SCS[:, TPC - 1:TPC]).then_inc(
                            prog.sems[("carc", "c")], 1)
                    with e.Else():
                        e.nop().then_inc(prog.sems[("carc", "c")], 1)
                    e.end_ifs()
                prog.raw("dve", car_copy, name="p1ccp")
            if DEBUG_TAPS and dc == 0:
                dma("sp", DBG["dbg_ss"][:, :], SSUM[:, :], r=[SSUM], name="dbgss")
                dma("sp", DBG["dbg_dt"][:, :], DT[:, :], r=[DT], name="dbgdt")
            def contrib_wait(e, prog, dc=dc, ms=car1_ms):
                preg = get_parity(e, prog)
                e.wait_ge(prog.sems[("dve", "c")], ms.num)
                t = get_reg(e, prog, "cwt2")
                e.reg_mov(t, 1)
                e.reg_sub(t, t, preg)
                e.reg_mul(t, t, NST * (dc + 1))
                e.wait_ge(prog.sems[("carc", "c")], t)
            prog.raw("pool", contrib_wait, name="p1cw2")
            dma("pool", CCCI[dc * 128:(dc + 1) * 128, :], CAR1[dc][:, :],
                w=[CCCI], name="p1ctx")
            cc = prog.add("pool", lambda e, dc=dc: e.collective_compute(
                "AllGather", ALU.bypass, replica_groups=RG,
                ins=[CCCI[dc * 128:(dc + 1) * 128, :]],
                outs=[CCCO[dc * 256:(dc + 1) * 256, :]]),
                r=[CCCI], w=[CCCO], kind="cc", name="p1cc")
            dma("pool", CAR2[dc][:, :], CCCO[dc * 256:dc * 256 + 128, :],
                r=[CCCO], w=[CAR2[dc]], name="p1crx", sem_key=("carry", dc))
            dma("pool", GW16[dc, :, :], gwT[dc, :, :], w=[GW16], name="wcvg")
            dma("pool", UW16[dc, :, :], uwT[dc, :, :], w=[UW16], name="wcvu")
            dma("pool", DW16[dc, :, :], dwT[dc, :, :], w=[DW16], name="wcvd")

            for tt in range(4):
                ps = PS[6]
                cmp("pe", lambda e, dc=dc, tt=tt, ps=ps: e.matmul(
                    ps[:, 0:512], WCTS[64:80, dc * 128:(dc + 1) * 128],
                    XDBC[64:80, tt * 512:(tt + 1) * 512], start=True, stop=True),
                    r=[WCTS, XDBC], w=[ps], name="p1cmm")
                t2 = SB5[2]
                cmp("dve", lambda e, tt=tt, ps=ps, t2=t2: e.tensor_mul(
                    t2[:, 0:512], SSUM[:, tt * 512:(tt + 1) * 512], ps[:, 0:512]),
                    r=[SSUM, ps], w=[t2], name="p1ymul")
                dp = DPRS[:, dc:dc + 1]
                cmp("dve", lambda e, dc=dc, tt=tt, t2=t2, dp=dp: e.scalar_tensor_tensor(
                    XN[dc][:, tt * 512:(tt + 1) * 512],
                    XA[dc][:, tt * 512:(tt + 1) * 512], dp, t2[:, 0:512],
                    ALU.mult, ALU.add),
                    r=[XA[dc], DPRS, t2], w=[XN[dc]], name="p1yc")

        if DEBUG_TAPS:
            for dc in range(DC):
                dma("pool", DBG["dbg_y"][dc * 128:(dc + 1) * 128, :], XN[dc][:, :],
                    r=[XN[dc]], name="dbgy")
        for oc in range(DC):
            wo = WO[0]
            dma("act", wo[:, :],
                WO16.ap().rearrange("(a p) ee -> p a ee", p=128)[:, :, oc * 128:(oc + 1) * 128],
                r=[WO16], w=[wo], name="p1wold")
            for tt in range(4):
                ps = PS[3] if tt % 2 == 0 else PS[7]
                for dc in range(DC):
                    cmp("pe", lambda e, tt=tt, dc=dc, wo=wo, ps=ps: e.matmul(
                        ps[:, 0:512], wo[:, dc * 128:(dc + 1) * 128],
                        XN[dc][:, tt * 512:(tt + 1) * 512],
                        start=(dc == 0), stop=(dc == DC - 1)),
                        r=[wo, XN[dc]], w=[ps], name="p1omm")
                szt = SB5[2 + tt % 2]
                xt = SB5[tt % 2]
                dma("sp", szt[:, 0:512],
                    SILZD[oc * 128:(oc + 1) * 128, tt * 512:(tt + 1) * 512],
                    r=[SILZD], w=[szt], name="p1szld")
                dma("sp", xt[:, 0:512],
                    xw[oc * 128:(oc + 1) * 128, 1 + tt * 512:1 + (tt + 1) * 512],
                    w=[xt], name="p1xld")
                cmp("dve", lambda e, ps=ps, szt=szt: e.tensor_mul(
                    szt[:, 0:512], ps[:, 0:512], szt[:, 0:512]),
                    r=[ps, szt], w=[szt], name="p1oz")
                cmp("dve", lambda e, szt=szt, xt=xt: e.tensor_add(
                    szt[:, 0:512], szt[:, 0:512], xt[:, 0:512]),
                    r=[szt, xt], w=[szt], name="p1res")
                dma("sp", dst[oc * 128:(oc + 1) * 128, tt * 512:(tt + 1) * 512],
                    szt[:, 0:512], r=[szt], w=[dst], name="p1xrst")
        st.close()

    def phase_moe(idx, dst, is_out):
        st = contextlib.ExitStack()
        sb = lambda nm, shp, dt=F32: st.enter_context(nc.sbuf_tensor(nm, shp, dt))
        HBT = [sb(f"m{idx}h{i}", [128, TPC // 2], BF16) for i in range(22)]
        GWB = [sb(f"m{idx}g{i}", [128, 8 * 256], BF16) for i in range(2)]
        UWB = [sb(f"m{idx}u{i}", [128, 8 * 256], BF16) for i in range(2)]
        DWB = [sb(f"m{idx}d{i}", [128, 11 * 128], BF16) for i in range(2)]
        WTE = sb(f"m{idx}wte", [128, 128])
        WTT = sb(f"m{idx}wtt", [8, TPC])
        WBC = sb(f"m{idx}wbc", [128, TPC])
        EYE = sb(f"m{idx}eye", [128, 128])
        dma("sp", EYE[:, :], eyeI[:, :], w=[EYE], sem_key=(f"su{idx}", "d"))
        SELS = sb(f"m{idx}sels", [NST, NST * 128])
        dma("sp", SELS[:, :], selI[:, :], w=[SELS], sem_key=(f"su{idx}", "d"))
        RWS = sb(f"m{idx}rws", [128, E * DC], BF16)
        dma("pool", RWS.ap().rearrange("p (a j) -> p a j", a=DC), rwT.ap().rearrange("(a p) j -> p a j", p=128), w=[RWS], sem_key=(f"sp{idx}", "d"))

        for dc in range(DC):
            dma("sp", XA[dc][:, :], XR[dc * 128:(dc + 1) * 128, :],
                r=[XR], w=[XA[dc]], name=f"m{idx}xald")
        emit_ln(lambda dc: XA[dc][:, :], lambda dc: XA[dc], None, f"m{idx}l")

        for ci in range(16):
            ps = PS[0]
            for dc in range(DC):
                cmp("pe", lambda e, ci=ci, dc=dc, ps=ps: e.matmul(
                    ps[:, 0:E], XN[dc][:, ci * 128:(ci + 1) * 128],
                    RWS[:, dc * E:(dc + 1) * E],
                    start=(dc == 0), stop=(dc == DC - 1)),
                    r=[XN[dc], RWS], w=[ps], name=f"m{idx}rt")
            t0, t1 = SB5[0], SB5[1]
            cmp("dve", lambda e, ps=ps, t0=t0: e.reduce_max(
                t0[:, 0:1], ps[:, 0:E], axis=AX.X, negate=True), r=[ps], w=[t0], small=True)
            cmp("act", lambda e, ps=ps, t0=t0, t1=t1: e.activation(
                out=t1[:, 0:E], in_=ps[:, 0:E], func=AF.Exp, bias=t0[:, 0:1]),
                r=[ps, t0], w=[t1], small=True)
            cmp("dve", lambda e, t1=t1, t0=t0: e.reduce_sum(
                t0[:, 1:2], t1[:, 0:E], axis=AX.X), r=[t1], w=[t0], small=True)
            cmp("dve", lambda e, t0=t0: e.reciprocal(t0[:, 2:3], t0[:, 1:2]),
                r=[t0], w=[t0], small=True)
            cmp("dve", lambda e, t1=t1, t0=t0: e.tensor_scalar(
                t1[:, 8:16], t1[:, 0:E], t0[:, 2:3], None, ALU.mult),
                r=[t1, t0], w=[t1], small=True)
            cmp("dve", lambda e, t1=t1, t0=t0: e.reduce_max(
                t0[:, 3:4], t1[:, 8:16], axis=AX.X), r=[t1], w=[t0], small=True)
            cmp("dve", lambda e, t1=t1, t0=t0: e.tensor_scalar(
                t1[:, 16:24], t1[:, 8:16], t0[:, 3:4], -1e30, ALU.is_ge, ALU.mult),
                r=[t1, t0], w=[t1], small=True)
            cmp("dve", lambda e, t1=t1: e.tensor_add(
                t1[:, 16:24], t1[:, 16:24], t1[:, 8:16]), r=[t1], w=[t1], small=True)
            cmp("dve", lambda e, t1=t1, t0=t0: e.reduce_max(
                t0[:, 4:5], t1[:, 16:24], axis=AX.X), r=[t1], w=[t0], small=True)
            cmp("dve", lambda e, t1=t1, t0=t0: e.tensor_scalar(
                t1[:, 24:32], t1[:, 8:16], t0[:, 4:5], None, ALU.is_ge),
                r=[t1, t0], w=[t1], small=True)
            cmp("dve", lambda e, ci=ci, t1=t1: e.tensor_mul(
                WTE[:, ci * 8:(ci + 1) * 8], t1[:, 24:32], t1[:, 8:16]),
                r=[t1], w=[WTE], name=f"m{idx}wte", small=True)
        for ci in range(16):
            ps = PS[1]
            cmp("pe", lambda e, ci=ci, ps=ps: e.transpose(
                ps[0:8, 0:128], WTE[:, ci * 8:(ci + 1) * 8], EYE[:, :]),
                r=[WTE, EYE], w=[ps], name=f"m{idx}tr")
            cmp("dve", lambda e, ci=ci, ps=ps: e.tensor_copy(
                WTT[:, ci * 128:(ci + 1) * 128], ps[0:8, 0:128]),
                r=[ps], w=[WTT], name=f"m{idx}trc", small=True)

        for ex in range(E):
            for tt in range(4):
                ps = PS[6]
                cmp("pe", lambda e, ex=ex, tt=tt, ps=ps: e.matmul(
                    ps[:, 0:512], SELS[0:8, ex * 128:(ex + 1) * 128],
                    WTT[:, tt * 512:(tt + 1) * 512], start=True, stop=True),
                    r=[WTT, SELS], w=[ps], name=f"m{idx}wb")
                cmp("dve", lambda e, tt=tt, ps=ps: e.tensor_copy(
                    WBC[:, tt * 512:(tt + 1) * 512], ps[:, 0:512]),
                    r=[ps], w=[WBC], name=f"m{idx}wbc")
            for th in range(2):
                for icp in range(11):
                    gw, uw = GWB[icp % 2], UWB[icp % 2]
                    dma("pool", gw[:, :],
                        GW16[ex, :, :].rearrange("(a p) i -> p a i", p=128)[:, :, icp * 256:(icp + 1) * 256],
                        r=[GW16], w=[gw], name=f"m{idx}gld")
                    dma("pool", uw[:, :],
                        UW16[ex, :, :].rearrange("(a p) i -> p a i", p=128)[:, :, icp * 256:(icp + 1) * 256],
                        r=[UW16], w=[uw], name=f"m{idx}uld")
                    for ic2 in range(2):
                        ic = icp * 2 + ic2
                        for tt2 in range(2):
                            tt = th * 2 + tt2
                            rot = (icp * 4 + ic2 * 2 + tt2) % 3
                            psg, psu = PS[[0, 1, 4][rot]], PS[[2, 3, 5][rot]]
                            for dc in range(DC):
                                cmp("pe", lambda e, gw=gw, ic2=ic2, tt=tt, dc=dc, psg=psg: e.matmul(
                                    psg[:, 0:512],
                                    gw[:, dc * 256 + ic2 * 128:dc * 256 + (ic2 + 1) * 128],
                                    XN[dc][:, tt * 512:(tt + 1) * 512],
                                    start=(dc == 0), stop=(dc == DC - 1)),
                                    r=[gw, XN[dc]], w=[psg], name=f"m{idx}gmm")
                            for dc in range(DC):
                                cmp("pe", lambda e, uw=uw, ic2=ic2, tt=tt, dc=dc, psu=psu: e.matmul(
                                    psu[:, 0:512],
                                    uw[:, dc * 256 + ic2 * 128:dc * 256 + (ic2 + 1) * 128],
                                    XN[dc][:, tt * 512:(tt + 1) * 512],
                                    start=(dc == 0), stop=(dc == DC - 1)),
                                    r=[uw, XN[dc]], w=[psu], name=f"m{idx}umm")
                            sg, t2 = SB5[tt2], SB5[2 + tt2]
                            cmp("act", lambda e, psg=psg, sg=sg: e.activation(
                                out=sg[:, 0:512], in_=psg[:, 0:512], func=AF.Silu),
                                r=[psg], w=[sg], name=f"m{idx}sig")
                            cmp("dve", lambda e, tt=tt, psu=psu, t2=t2: e.tensor_mul(
                                t2[:, 0:512], psu[:, 0:512], WBC[:, tt * 512:(tt + 1) * 512]),
                                r=[psu, WBC], w=[t2], name=f"m{idx}uw")
                            cmp("dve", lambda e, ic=ic, tt2=tt2, sg=sg, t2=t2: e.tensor_mul(
                                HBT[ic][:, tt2 * 512:(tt2 + 1) * 512], t2[:, 0:512], sg[:, 0:512]),
                                r=[t2, sg], w=[HBT[ic]], name=f"m{idx}hm")
                for oc in range(DC):
                    for tt2 in range(2):
                        tt = th * 2 + tt2
                        ps = PS[6 + tt2]
                        for ih in range(2):
                            dw = DWB[ih]
                            if tt2 == 0:
                                dma("act", dw[:, :],
                                    DW16[ex, ih * 11 * 128:(ih + 1) * 11 * 128, :].rearrange(
                                        "(a p) h -> p a h", p=128)[:, :, oc * 128:(oc + 1) * 128],
                                    r=[DW16], w=[dw], name=f"m{idx}dld")
                            for ic2 in range(11):
                                ic = ih * 11 + ic2
                                cmp("pe", lambda e, dw=dw, ic=ic, ic2=ic2, tt2=tt2, ps=ps: e.matmul(
                                    ps[:, 0:512], dw[:, ic2 * 128:(ic2 + 1) * 128],
                                    HBT[ic][:, tt2 * 512:(tt2 + 1) * 512],
                                    start=(ic == 0), stop=(ic == 21)),
                                    r=[dw, HBT[ic]], w=[ps], name=f"m{idx}dmm")
                        cmp("dve", lambda e, oc=oc, tt=tt, ps=ps: e.tensor_add(
                            XA[oc][:, tt * 512:(tt + 1) * 512],
                            XA[oc][:, tt * 512:(tt + 1) * 512], ps[:, 0:512]),
                            r=[ps, XA[oc]], w=[XA[oc]], name=f"m{idx}ya")
        for dc in range(DC):
            dma("sp", dst[dc * 128:(dc + 1) * 128, :], XA[dc][:, :],
                r=[XA[dc]], w=[dst], name=f"m{idx}st")
        st.close()

    def phase3(dst, is_out):
        st = contextlib.ExitStack()
        sb = lambda nm, shp, dt=F32: st.enter_context(nc.sbuf_tensor(nm, shp, dt))
        XNP = [sb(f"a3xnp{i}", [128, TPC], BF16) for i in range(DC)]
        W1B = [sb("a3w1", [128, 8 * 384], BF16)]
        WUB = sb("a3wu", [128, 8 * 512], BF16)
        KH = sb("a3kh", [128, 2 * TPC], BF16)
        VH = [sb(f"a3vh{i}", [128, TPC], BF16) for i in range(2)]
        QHP = [sb(f"a3qhp{h}", [128, TPC], BF16) for h in range(2)]
        cmp("dve", lambda e: e.memset(QHP[0][64:128, :], 0.0), w=[QHP[0]])
        cmp("dve", lambda e: e.memset(QHP[1][0:64, :], 0.0), w=[QHP[1]])
        AT = [sb(f"a3at{i}", [128, TPC], BF16) for i in range(2)]
        UT = sb("a3ut", [128, TPC])
        B1S = sb("b1s", [128, 32])
        B1VB = sb("b1vb", [1, D], BF16)
        B2S = sb("b2s", [128, DC])
        GAMS = sb("gams", [128, DC])
        BETS = sb("bets", [128, DC])
        dma("sp", B1S.ap().rearrange("p (a k) -> p a k", a=32), b1.ap().rearrange("(a p) k -> p a k", p=128), w=[B1S], slow=True, sem_key=("su3", "d"))
        dma("pool", B1VB[:, :], b1[1024:2048, 0:1].rearrange("a k -> k a"), w=[B1VB], sem_key=("sp3", "d"))
        dma("sp", B2S.ap().rearrange("p (a k) -> p a k", a=DC), b2.ap().rearrange("(a p) k -> p a k", p=128), w=[B2S], slow=True, sem_key=("su3", "d"))
        dma("sp", GAMS.ap().rearrange("p (a k) -> p a k", a=DC), gamma.ap().rearrange("(a p) k -> p a k", p=128), w=[GAMS], slow=True, sem_key=("su3", "d"))
        dma("sp", BETS.ap().rearrange("p (a k) -> p a k", a=DC), beta.ap().rearrange("(a p) k -> p a k", p=128), w=[BETS], slow=True, sem_key=("su3", "d"))

        for dc in range(DC):
            dma("sp", XA[dc][:, :], XR[dc * 128:(dc + 1) * 128, :],
                r=[XR], w=[XA[dc]], name="a3xald")
        emit_ln(lambda dc: XA[dc][:, :], lambda dc: XA[dc], None, "a3l")

        for dc in range(DC):
            dma("pool", CCXI[dc * 128:(dc + 1) * 128, :], XN[dc][:, :],
                r=[XN[dc]], w=[CCXI], name="a3ctx")
        ccx = prog.add("pool", lambda e: e.collective_compute(
            "AllGather", ALU.bypass, replica_groups=RG,
            ins=[CCXI.ap()], outs=[CCXO.ap()]),
            r=[CCXI], w=[CCXO], kind="cc", name="a3cc")

        def xnp_load(e, prog):
            preg = get_parity(e, prog)
            t = get_reg(e, prog, "xoff")
            e.wait_ge(prog.sems[ccx.sem_key], ccx.num)
            e.reg_mov(t, 1)
            e.reg_sub(t, t, preg)
            e.reg_mul(t, t, 1024 * TPC)
            t2 = get_reg(e, prog, "xoff2")
            for dc in range(DC):
                e.reg_add(t2, t, dc * 128 * TPC)
                e.dma_start(out=XNP[dc][:, :],
                            in_=bass.AP(CCXO, t2, [[TPC, 128], [1, TPC]])).then_inc(
                    prog.sems[("xnp", "d")], 16)
        prog.raw("sp", xnp_load, name="a3xnpld")
        prog.sem_keys.add(("xnp", "d"))

        for uh in range(2):
            dma("pool", WUB[:, :],
                W1.ap().rearrange("(a p) ee -> p a ee", p=128)[:, :, uh * 512:(uh + 1) * 512],
                w=[WUB], name="a3wuld")
            for uc2 in range(4):
                uc = uh * 4 + uc2
                for tt in range(4):
                    ps = PS[uc % 2]
                    for dc in range(DC):
                        cmp("pe", lambda e, uc2=uc2, tt=tt, dc=dc, ps=ps: e.matmul(
                            ps[:, 0:512], WUB[:, dc * 512 + uc2 * 128:dc * 512 + (uc2 + 1) * 128],
                            XN[dc][:, tt * 512:(tt + 1) * 512],
                            start=(dc == 0), stop=(dc == DC - 1)),
                            r=[WUB, XN[dc]], w=[ps], name="a3umm")
                    ub = SB5[2 + tt % 2]
                    cmp("act", lambda e, uc=uc, ps=ps, ub=ub: e.activation(
                        out=ub[:, 0:512], in_=ps[:, 0:512], func=AF.Silu,
                        bias=B1S[:, uc:uc + 1]),
                        r=[ps, B1S], w=[ub], name="a3usil")
                    dma("sp", UD[uc * 128:(uc + 1) * 128, tt * 512:(tt + 1) * 512],
                        ub[:, 0:512], r=[ub], w=[UD], name="a3ust")

        def xnp_wait(e, prog):
            e.wait_ge(prog.sems[("xnp", "d")], 16 * DC)

        for hp in range(8):
            w1b = W1B[0]
            qc0, kc0, vc0 = 2048 + hp * 128, 3072 + hp * 128, 1024 + hp * 128
            w1r = W1.ap().rearrange("(a p) ee -> p a ee", p=128)
            w1v = w1b[:, :].rearrange("p (a j) -> p a j", j=384)
            dma("pool", w1v[:, :, 0:128], w1r[:, :, qc0:qc0 + 128], w=[w1b], name="a3w1q")
            dma("pool", w1v[:, :, 128:256], w1r[:, :, kc0:kc0 + 128], w=[w1b], name="a3w1k")
            dma("pool", w1v[:, :, 256:384], w1r[:, :, vc0:vc0 + 128], w=[w1b], name="a3w1v")

            for tt in range(4):
                ps = PS[tt % 2]
                for dc in range(DC):
                    cmp("pe", lambda e, tt=tt, dc=dc, w1b=w1b, ps=ps: e.matmul(
                        ps[:, 0:512], w1b[:, dc * 384:dc * 384 + 128],
                        XN[dc][:, tt * 512:(tt + 1) * 512],
                        start=(dc == 0), stop=(dc == DC - 1)),
                        r=[w1b, XN[dc]], w=[ps], name="a3qmm")
                for h2 in range(2):
                    pb = 64 * h2
                    cmp("act", lambda e, hp=hp, tt=tt, ps=ps, pb=pb, h2=h2: e.activation(
                        out=QHP[h2][pb:pb + 64, tt * 512:(tt + 1) * 512],
                        in_=ps[pb:pb + 64, 0:512], func=AF.Silu,
                        bias=B1S[pb:pb + 64, 16 + hp:16 + hp + 1]),
                        r=[ps, B1S], w=[QHP[h2]], name="a3qsil")
            if hp == 0:
                prog.raw("pe", xnp_wait, name="a3xnpw")
            for half in range(2):
                XSRC = XN if half == 0 else XNP
                for tt in range(4):
                    ps = PS[2 + tt % 2]
                    for dc in range(DC):
                        cmp("pe", lambda e, tt=tt, dc=dc, w1b=w1b, ps=ps, XSRC=XSRC: e.matmul(
                            ps[:, 0:512], w1b[:, dc * 384 + 128:dc * 384 + 256],
                            XSRC[dc][:, tt * 512:(tt + 1) * 512],
                            start=(dc == 0), stop=(dc == DC - 1)),
                            r=[w1b, XSRC[dc]], w=[ps], name="a3kmm")
                    cmp("act", lambda e, hp=hp, half=half, tt=tt, ps=ps: e.activation(
                        out=KH[:, half * TPC + tt * 512:half * TPC + (tt + 1) * 512],
                        in_=ps[:, 0:512], func=AF.Silu,
                        bias=B1S[:, 24 + hp:24 + hp + 1]),
                        r=[ps, B1S], w=[KH], name="a3ksil")
            for half in range(2):
                XSRC = XN if half == 0 else XNP
                vh = VH[half]
                for kc in range(16):
                    ps = PS[2 + kc % 2]
                    for dc in range(DC):
                        cmp("pe", lambda e, kc=kc, dc=dc, w1b=w1b, ps=ps, XSRC=XSRC: e.matmul(
                            ps[:, 0:128], XSRC[dc][:, kc * 128:(kc + 1) * 128],
                            w1b[:, dc * 384 + 256:dc * 384 + 384],
                            start=(dc == 0), stop=False),
                            r=[w1b, XSRC[dc]], w=[ps], name="a3vmm")
                    cmp("pe", lambda e, hp=hp, ps=ps: e.matmul(
                        ps[:, 0:128], ONESRB[0:1, :], B1VB[0:1, hp * 128:(hp + 1) * 128],
                        start=False, stop=True),
                        r=[ONESRB, B1VB], w=[ps], name="a3vb")
                    cmp("act", lambda e, kc=kc, vh=vh, ps=ps: e.activation(
                        out=vh[:, kc * 128:(kc + 1) * 128], in_=ps[:, 0:128], func=AF.Silu),
                        r=[ps], w=[vh], name="a3vsil")
            dma("sp", UT[:, :], UD[hp * 128:(hp + 1) * 128, :], r=[UD], w=[UT],
                name="a3uld")
            for h2 in range(2):
                pb = 64 * h2
                for kc in range(33):
                    if kc < 32:
                        at = AT[kc % 2]
                        for qt in range(4):
                            ps = PS[(kc * 4 + qt) % 2]
                            cmp("pe", lambda e, kc=kc, qt=qt, h2=h2, ps=ps: e.matmul(
                                ps[:, 0:512], KH[:, kc * 128:(kc + 1) * 128],
                                QHP[h2][:, qt * 512:(qt + 1) * 512],
                                start=True, stop=True),
                                r=[KH, QHP[h2]], w=[ps], name="a3smm")
                            cmp("act", lambda e, qt=qt, at=at, ps=ps: e.activation(
                                out=at[:, qt * 512:(qt + 1) * 512], in_=ps[:, 0:512],
                                func=AF.Silu),
                                r=[ps], w=[at], name="a3ssil")
                    if kc >= 1:
                        ka = kc - 1
                        at = AT[ka % 2]
                        half, kci = ka // 16, ka % 16
                        vh = VH[half]
                        for qt in range(4):
                            psv = PS[4 + qt]
                            vsl = vh[:, kci * 128:(kci + 1) * 128].rearrange(
                                "p (a j) -> p a j", j=64)[:, h2, :]
                            cmp("pe", lambda e, ka=ka, qt=qt, pb=pb, at=at, vsl=vsl, psv=psv: e.matmul(
                                psv[pb:pb + 64, 0:512], vsl, at[:, qt * 512:(qt + 1) * 512],
                                start=(ka == 0), stop=(ka == 31)),
                                r=[vh, at], w=[psv], name="a3avmm")
                for qt in range(4):
                    psv = PS[4 + qt]
                    cmp("dve", lambda e, hp=hp, h2=h2, qt=qt, pb=pb, psv=psv: e.scalar_tensor_tensor(
                        XA[hp][pb:pb + 64, qt * 512:(qt + 1) * 512],
                        psv[pb:pb + 64, 0:512], CONSTS[pb:pb + 64, 1:2],
                        UT[pb:pb + 64, qt * 512:(qt + 1) * 512],
                        ALU.mult, ALU.mult),
                        r=[psv, UT, CONSTS], w=[XA[hp]], name="a3avev")

        emit_ln(lambda dc: XA[dc][:, :], lambda dc: XA[dc], (GAMS, BETS), "a4l")
        for oh in range(2):
            dma("pool", WUB[:, :],
                W2.ap().rearrange("(a p) ee -> p a ee", p=128)[:, :, oh * 512:(oh + 1) * 512],
                w=[WUB], name="a3w2ld")
            for oc2 in range(4):
                oc = oh * 4 + oc2
                for tt in range(4):
                    ps = PS[oc % 2]
                    for dc in range(DC):
                        cmp("pe", lambda e, oc2=oc2, tt=tt, dc=dc, ps=ps: e.matmul(
                            ps[:, 0:512], WUB[:, dc * 512 + oc2 * 128:dc * 512 + (oc2 + 1) * 128],
                            XN[dc][:, tt * 512:(tt + 1) * 512],
                            start=(dc == 0), stop=(dc == DC - 1)),
                            r=[WUB, XN[dc]], w=[ps], name="a3w2mm")
                    xt = SB5[tt % 2]
                    dma("sp", xt[:, 0:512],
                        XR[oc * 128:(oc + 1) * 128, tt * 512:(tt + 1) * 512],
                        r=[XR], w=[xt], name="a3rld")
                    ot = SB5[2 + tt % 2]
                    cmp("dve", lambda e, oc=oc, ps=ps, xt=xt, ot=ot: e.scalar_tensor_tensor(
                        ot[:, 0:512], ps[:, 0:512], B2S[:, oc:oc + 1], xt[:, 0:512],
                        ALU.add, ALU.add),
                        r=[ps, B2S, xt], w=[ot], name="a3o")
                    dma("sp", dst[oc * 128:(oc + 1) * 128, tt * 512:(tt + 1) * 512],
                        ot[:, 0:512], r=[ot], w=[dst], name="a3st")
        st.close()

    dsts = [XR, XR, XR, out]
    dsts[n_sub - 1] = out
    phase1(dsts[0], n_sub == 1)
    if n_sub >= 2:
        prog.barrier()
        phase_moe(2, dsts[1], n_sub == 2)
    if n_sub >= 3:
        prog.barrier()
        phase3(dsts[2], n_sub == 3)
    if n_sub >= 4:
        prog.barrier()
        phase_moe(4, dsts[3], n_sub == 4)

    prog.raw("sp", lambda e, p: e.wait_ge(p.sems[("b", id(out))],
                                          16 * p.dma_counts[("b", id(out))]),
             name="finalwait")

    prog.finalize_numbering()
    keys = set(prog.sem_keys) | {("xnp", "d"), ("bar", "c"), ("carc", "c")}
    stack = contextlib.ExitStack()
    sems = {k: stack.enter_context(nc.semaphore(f"s_{k[0]}_{k[1]}")) for k in keys}
    with nc.Block() as block:
        prog.emit(block, sems)
    stack.close()
    return nc


def _host_prep(inputs):
    x = np.asarray(inputs["L0_attn_a_x"], dtype=np.float32)
    C = np.ascontiguousarray
    WinT = C(np.asarray(inputs["L0_attn_a_W_in"], np.float32).T)
    convw = C(np.asarray(inputs["L0_attn_a_conv_w"], np.float32))
    WxT0 = np.asarray(inputs["L0_attn_a_W_x"], np.float32).T
    perm = list(range(R, R + NST)) + list(range(R + NST, R + 2 * NST)) + list(range(R))
    WxT = C(WxT0[:, perm])
    WdtT = C(np.asarray(inputs["L0_attn_a_W_dt"], np.float32).T)
    bdt = C(np.asarray(inputs["L0_attn_a_b_dt"], np.float32).reshape(D, 1))
    Alog = C(np.asarray(inputs["L0_attn_a_A_log"], np.float32))
    Dprm = C(np.asarray(inputs["L0_attn_a_D_param"], np.float32).reshape(D, 1))
    WCT = C(np.asarray(inputs["L0_attn_a_W_C"], np.float32).T)
    WoutT = C(np.asarray(inputs["L0_attn_a_W_out"], np.float32).T)
    W1 = C(np.asarray(inputs["L0_attn_b_W1"], np.float32))
    b1 = C(np.asarray(inputs["L0_attn_b_b1"], np.float32).reshape(4 * D, 1))
    W2 = C(np.asarray(inputs["L0_attn_b_W2"], np.float32))
    b2 = C(np.asarray(inputs["L0_attn_b_b2"], np.float32).reshape(D, 1))
    gamma = C(np.asarray(inputs["L0_attn_b_gamma"], np.float32).reshape(D, 1))
    beta = C(np.asarray(inputs["L0_attn_b_beta"], np.float32).reshape(D, 1))
    rwT = C(np.asarray(inputs["L0_ffn_router_w"], np.float32).T)
    gwT = C(np.asarray(inputs["L0_ffn_gate_w"], np.float32).transpose(0, 2, 1))
    uwT = C(np.asarray(inputs["L0_ffn_up_w"], np.float32).transpose(0, 2, 1))
    dwT = C(np.asarray(inputs["L0_ffn_down_w"], np.float32).transpose(0, 2, 1))
    eye = np.eye(128, dtype=np.float32)
    sel = np.zeros((NST, NST * 128), np.float32)
    for n in range(NST):
        sel[n, n * 128:(n + 1) * 128] = 1.0

    in_maps = []
    for c in range(8):
        b, p = c // 2, c % 2
        xbT = C(x[b].T)
        xwin = np.zeros((D, TPC + 2), np.float32)
        t0 = p * TPC
        xwin[:, 1:TPC + 1] = xbT[:, t0:t0 + TPC]
        if t0 - 1 >= 0:
            xwin[:, 0] = xbT[:, t0 - 1]
        if t0 + TPC < T:
            xwin[:, TPC + 1] = xbT[:, t0 + TPC]
        cfgv = np.zeros((1, 8), np.int32)
        cfgv[0, 0] = p
        in_maps.append(dict(
            xw=xwin, WinT=WinT, convw=convw, WxT=WxT, WdtT=WdtT, bdt=bdt,
            Alog=Alog, Dprm=Dprm, WCT=WCT, WoutT=WoutT, W1=W1, b1=b1, W2=W2,
            b2=b2, gamma=gamma, beta=beta, rwT=rwT, gwT=gwT, uwT=uwT, dwT=dwT,
            cfg=cfgv, eyeI=eye, selI=sel))
    return in_maps


def kernel(**inputs):
    in_maps = _host_prep(inputs)
    nc = build_program(N_SUBBLOCKS)
    res = run_bass_kernel_spmd(nc, in_maps, core_ids=list(range(8)))
    outs = [res.results[c]["out"].T for c in range(8)]
    full = np.stack([np.concatenate([outs[2 * b], outs[2 * b + 1]], axis=0)
                     for b in range(B)], axis=0)
    return full.astype(np.float32)


# version 13
# speedup vs baseline: 1.1099x; 1.1099x over previous
import contextlib
import numpy as np
import concourse.bass as bass
import concourse.mybir as mybir
from concourse.bass_utils import run_bass_kernel_spmd

F32 = mybir.dt.float32
BF16 = mybir.dt.bfloat16
I32 = mybir.dt.int32
AF = mybir.ActivationFunctionType
ALU = mybir.AluOpType
AX = mybir.AxisListType

B, T, D, NST, R = 4, 4096, 1024, 16, 64
E, I_FF = 8, 2816
NH, DH = 16, 64
TPC = 2048
DC = D // 128
LN_EPS = 1e-5
RG = [[0, 1], [2, 3], [4, 5], [6, 7]]

N_SUBBLOCKS = 4
DEBUG_TAPS = False
PARANOID_SYNC = False
FULL_SYNC = False


class Op:
    __slots__ = ("eng", "fn", "kind", "deps", "inc", "sem_key", "num", "name",
                 "small", "snapshot")

    def __init__(self, eng, fn, kind, sem_key, name, small=False):
        self.eng, self.fn, self.kind = eng, fn, kind
        self.sem_key, self.name = sem_key, name
        self.deps = []
        self.inc = False
        self.num = None
        self.small = small
        self.snapshot = None


class Prog:
    ENGS = ("pe", "act", "dve", "pool", "sp")
    BLOCK_ATTR = {"pe": "tensor", "act": "scalar", "dve": "vector",
                  "pool": "gpsimd", "sp": "sync"}

    def __init__(self, nc):
        self.nc = nc
        self.ops = {e: [] for e in self.ENGS}
        self.last_w = {}
        self.readers = {}
        self.sems = {}
        self.sem_keys = set()
        self.dma_counts = {}
        self.eng_dma = {e: {} for e in self.ENGS}
        self.cc_idx = 0
        self.n_barriers = 0
        self.regs = {}

    def add(self, eng, fn, r=(), w=(), kind="cmp", name="", sem_key=None, small=False):
        if kind == "dma":
            if sem_key is None:
                sem_key = ("b", id(w[0])) if w else (eng, "d")
        elif kind == "cc":
            self.cc_idx += 1
            sem_key = ("cc", self.cc_idx)
        elif sem_key is None:
            sem_key = (eng, "c")
        op = Op(eng, fn, kind, sem_key, name, small=small)
        if kind == "dma":
            self.dma_counts[sem_key] = self.dma_counts.get(sem_key, 0) + 1
            op.num = 16 * self.dma_counts[sem_key]
            self.eng_dma[eng][sem_key] = op.num
            op.inc = True
        elif kind == "cc":
            op.num = 1
            self.eng_dma[eng][sem_key] = op.num
            op.inc = True

        def want_edge(prod):
            if prod.eng != eng:
                return True
            if prod.kind == "cmp" and kind == "cmp":
                if eng == "pe":
                    return False
                return prod.small or FULL_SYNC
            if prod.kind == "dma" and kind == "dma" and prod.sem_key == sem_key:
                return False
            if prod.kind == "cmp" and kind in ("dma", "cc"):
                return False
            return True

        def dep_of(prod):
            if prod.kind in ("dma", "cc"):
                if prod.kind == "dma":
                    return ("fixed", prod.sem_key, 16 * self.dma_counts[prod.sem_key])
                return ("fixed", prod.sem_key, prod.num)
            return ("op", prod)

        added = set()
        def add_dep(prod):
            if prod is op or id(prod) in added:
                return
            if want_edge(prod):
                op.deps.append(dep_of(prod))
                added.add(id(prod))
                if prod.kind == "cmp":
                    prod.inc = True
        for t in r:
            lw = self.last_w.get(id(t))
            if lw is not None:
                add_dep(lw)
        for t in w:
            lw = self.last_w.get(id(t))
            if lw is not None:
                add_dep(lw)
            for rd in self.readers.get(id(t), ()):
                add_dep(rd)
        for t in w:
            self.last_w[id(t)] = op
            self.readers[id(t)] = []
        for t in r:
            self.readers.setdefault(id(t), []).append(op)
        self.ops[eng].append(op)
        self.sem_keys.add(sem_key)
        return op

    def raw(self, eng, fn, name="raw"):
        op = Op(eng, fn, "raw", None, name)
        self.ops[eng].append(op)
        return op

    def barrier(self):
        self.n_barriers += 1
        self.sem_keys.add(("bar", "c"))
        nb = self.n_barriers
        for eng in self.ENGS:
            op = Op(eng, None, "barsig", ("bar", "c"), f"barsig{nb}")
            op.inc = True
            op.snapshot = dict(self.eng_dma[eng])
            self.ops[eng].append(op)
        for eng in self.ENGS:
            self.ops[eng].append(Op(eng, None, "barwait", None, f"barwait{nb}"))
        self.last_w.clear()
        self.readers.clear()

    def finalize_numbering(self):
        counts = {k: 0 for k in self.sem_keys}
        for k, v in self.dma_counts.items():
            counts[k] = 16 * v
        counts[("cc", "n")] = self.cc_idx
        for eng in self.ENGS:
            for op in self.ops[eng]:
                if op.kind == "barsig":
                    counts[op.sem_key] += 1
                    op.num = counts[op.sem_key]
                elif op.kind == "cmp" and op.inc:
                    counts[op.sem_key] = counts.get(op.sem_key, 0) + 1
                    op.num = counts[op.sem_key]
        self.final_counts = counts

    def emit(self, block, sems):
        self.sems = sems
        for eng in self.ENGS:
            def body(e, eng=eng):
                watermark = {}
                for op in self.ops[eng]:
                    if op.kind == "barsig":
                        for sk, tot in op.snapshot.items():
                            if watermark.get(sk, 0) < tot:
                                e.wait_ge(sems[sk], tot)
                                watermark[sk] = tot
                        e.nop().then_inc(sems[("bar", "c")], 1)
                        continue
                    if op.kind == "barwait":
                        e.wait_ge(sems[("bar", "c")], 5 * int(op.name[7:]))
                        continue
                    if op.kind == "raw":
                        op.fn(e, self)
                        continue
                    for d in op.deps:
                        if d[0] == "fixed":
                            sk, v = d[1], d[2]
                        else:
                            sk, v = d[1].sem_key, d[1].num
                            if v is None:
                                continue
                        if watermark.get(sk, 0) < v:
                            e.wait_ge(sems[sk], v)
                            watermark[sk] = v
                    inst = op.fn(e)
                    if op.inc:
                        inst.then_inc(sems[op.sem_key],
                                      16 if op.kind == "dma" else 1)
            getattr(block, self.BLOCK_ATTR[eng])(body)


def build_program(n_sub):
    nc = bass.Bass("TRN2", target_bir_lowering=False)

    xw = nc.dram_tensor("xw", [D, TPC + 2], F32, kind="ExternalInput")
    WinT = nc.dram_tensor("WinT", [D, 2 * D], F32, kind="ExternalInput")
    convw = nc.dram_tensor("convw", [D, 3], F32, kind="ExternalInput")
    WxT = nc.dram_tensor("WxT", [D, R + 2 * NST], F32, kind="ExternalInput")
    WdtT = nc.dram_tensor("WdtT", [R, D], F32, kind="ExternalInput")
    bdt = nc.dram_tensor("bdt", [D, 1], F32, kind="ExternalInput")
    Alog = nc.dram_tensor("Alog", [D, NST], F32, kind="ExternalInput")
    Dprm = nc.dram_tensor("Dprm", [D, 1], F32, kind="ExternalInput")
    WCT = nc.dram_tensor("WCT", [NST, D], F32, kind="ExternalInput")
    WoutT = nc.dram_tensor("WoutT", [D, D], F32, kind="ExternalInput")
    W1 = nc.dram_tensor("W1", [D, 4 * D], F32, kind="ExternalInput")
    b1 = nc.dram_tensor("b1", [4 * D, 1], F32, kind="ExternalInput")
    W2 = nc.dram_tensor("W2", [D, D], F32, kind="ExternalInput")
    b2 = nc.dram_tensor("b2", [D, 1], F32, kind="ExternalInput")
    gamma = nc.dram_tensor("gamma", [D, 1], F32, kind="ExternalInput")
    beta = nc.dram_tensor("beta", [D, 1], F32, kind="ExternalInput")
    rwT = nc.dram_tensor("rwT", [D, E], F32, kind="ExternalInput")
    gwT = nc.dram_tensor("gwT", [E, D, I_FF], F32, kind="ExternalInput")
    uwT = nc.dram_tensor("uwT", [E, D, I_FF], F32, kind="ExternalInput")
    dwT = nc.dram_tensor("dwT", [E, I_FF, D], F32, kind="ExternalInput")
    cfg = nc.dram_tensor("cfg", [1, 8], I32, kind="ExternalInput")
    eyeI = nc.dram_tensor("eyeI", [128, 128], F32, kind="ExternalInput")
    selI = nc.dram_tensor("selI", [NST, NST * 128], F32, kind="ExternalInput")

    out = nc.dram_tensor("out", [D, TPC], F32, kind="ExternalOutput")

    XR = nc.dram_tensor("XR", [D, TPC], F32)
    SILZD = nc.dram_tensor("SILZD", [D, TPC], F32)
    UD = nc.dram_tensor("UD", [D, TPC], F32)
    CCCI = nc.dram_tensor("CCCI", [D, NST], F32)
    CCCO = nc.dram_tensor("CCCO", [2 * D, NST], F32)
    CCXI = nc.dram_tensor("CCXI", [D, TPC], BF16)
    CCXO = nc.dram_tensor("CCXO", [2 * D, TPC], BF16)
    GW16 = nc.dram_tensor("GW16", [E, D, I_FF], BF16)
    WO16 = nc.dram_tensor("WO16", [D, D], BF16)
    UW16 = nc.dram_tensor("UW16", [E, D, I_FF], BF16)
    DW16 = nc.dram_tensor("DW16", [E, I_FF, D], BF16)
    W116 = nc.dram_tensor("W116", [D, 4 * D], BF16)
    W216 = nc.dram_tensor("W216", [D, D], BF16)
    if DEBUG_TAPS:
        DBG = {nm: nc.dram_tensor(nm, [D, TPC], F32, kind="ExternalOutput")
               for nm in ("dbg_xn", "dbg_xs", "dbg_y")}
        DBG["dbg_xd"] = nc.dram_tensor("dbg_xd", [96, TPC], F32, kind="ExternalOutput")
        DBG["dbg_ss"] = nc.dram_tensor("dbg_ss", [128, TPC], F32, kind="ExternalOutput")
        DBG["dbg_dt"] = nc.dram_tensor("dbg_dt", [128, TPC], F32, kind="ExternalOutput")
        DBG["dbg_xnh"] = nc.dram_tensor("dbg_xnh", [D, 2], F32, kind="ExternalOutput")
        DBG["dbg_rows"] = nc.dram_tensor("dbg_rows", [1, 512 * 3], F32, kind="ExternalOutput")
        DBG["dbg_xc"] = nc.dram_tensor("dbg_xc", [128, TPC + 2], F32, kind="ExternalOutput")

    prog = Prog(nc)

    PS = [nc.alloc_psum_tensor(f"ps{i}", [128, 512], F32) for i in range(8)]
    XN = [nc.alloc_sbuf_tensor(f"xn{i}", [128, TPC], BF16) for i in range(DC)]
    XNH = [nc.alloc_sbuf_tensor(f"xnh{i}", [128, 2], BF16) for i in range(DC)]
    XA = [nc.alloc_sbuf_tensor(f"xa{i}", [128, TPC], F32) for i in range(DC)]
    SB5 = [nc.alloc_sbuf_tensor(f"sb5{i}", [128, 512], F32) for i in range(4)]
    ROWS = nc.alloc_sbuf_tensor("rows", [1, 512 * 3], F32)
    ONESC = nc.alloc_sbuf_tensor("onesc", [128, 2], BF16)
    ONESR = nc.alloc_sbuf_tensor("onesr", [1, 128], F32)
    ONESRB = nc.alloc_sbuf_tensor("onesrb", [1, 128], BF16)
    CFGS = nc.alloc_sbuf_tensor("cfgs", [1, 8], I32)
    EYE = nc.alloc_sbuf_tensor("eye", [128, 128], F32)
    CONSTS = nc.alloc_sbuf_tensor("consts", [128, 2], F32)
    nc.const_aps.register(F32, 1e-5, CONSTS[:, 0:1])

    def dma(eng, outap, inap, r=(), w=(), name="dma", sem_key=None, slow=False):
        return prog.add(eng, lambda e, o=outap, i=inap, s=slow: e.dma_start(
            out=o, in_=i, allow_slow_non_contiguous=s),
            r=r, w=w, kind="dma", name=name, sem_key=sem_key)

    def cmp(eng, fn, r=(), w=(), name="", small=False, sem_key=None):
        return prog.add(eng, fn, r=r, w=w, kind="cmp", name=name, small=small,
                        sem_key=sem_key)

    def rowslc(j, w=512):
        return ROWS[0:1, j * 512:j * 512 + w]

    cfg_ld = dma("sp", CFGS[:, :], cfg[:, :], w=[CFGS], name="cfgld",
                 sem_key=("su0", "d"))
    cmp("dve", lambda e: e.memset(ONESC[:, :], 1.0), w=[ONESC])
    cmp("dve", lambda e: e.memset(CONSTS[:, 0:1], 1e-5), w=[CONSTS])
    cmp("dve", lambda e: e.memset(CONSTS[:, 1:2], 1.0 / T), w=[CONSTS])
    cmp("dve", lambda e: e.memset(ONESR[:, :], 1.0), w=[ONESR])
    cmp("dve", lambda e: e.memset(ONESRB[:, :], 1.0), w=[ONESRB])

    def get_parity(e, prog):
        key = (str(e.engine), "parity")
        if key not in prog.regs:
            reg = e.alloc_register("parity")
            e.wait_ge(prog.sems[("su0", "d")], 16 * prog.dma_counts[("su0", "d")])
            e.reg_load(reg, CFGS[0:1, 0:1])
            prog.regs[key] = reg
        return prog.regs[key]

    def get_reg(e, prog, name):
        key = (str(e.engine), name)
        if key not in prog.regs:
            prog.regs[key] = e.alloc_register(name)
        return prog.regs[key]

    def emit_ln(src, srcT, gb, tag):
        with contextlib.ExitStack() as st:
            XBB = [st.enter_context(nc.sbuf_tensor(f"{tag}xb{i}", [128, 512], BF16))
                   for i in range(2)]
            SQB = [st.enter_context(nc.sbuf_tensor(f"{tag}sq{i}", [128, 512], BF16))
                   for i in range(2)]
            for tt in range(4):
                sl = slice(tt * 512, (tt + 1) * 512)
                for dc in range(DC):
                    xb, sq = XBB[dc % 2], SQB[dc % 2]
                    cmp("act", lambda e, dc=dc, sl=sl, xb=xb: e.activation(
                        out=xb[:, 0:512], in_=src(dc)[:, sl], func=AF.Copy),
                        r=[srcT(dc)], w=[xb], name=f"{tag}cp")
                    cmp("act", lambda e, dc=dc, sl=sl, sq=sq: e.activation(
                        out=sq[:, 0:512], in_=src(dc)[:, sl], func=AF.Square),
                        r=[srcT(dc)], w=[sq], name=f"{tag}sq")
                    cmp("pe", lambda e, dc=dc, xb=xb: e.matmul(
                        PS[6][0:1, 0:512], ONESC[:, 0:1], xb[:, 0:512],
                        start=(dc == 0), stop=(dc == DC - 1)),
                        r=[xb, ONESC], w=[PS[6]], name=f"{tag}s1")
                    cmp("pe", lambda e, dc=dc, sq=sq: e.matmul(
                        PS[7][0:1, 0:512], ONESC[:, 0:1], sq[:, 0:512],
                        start=(dc == 0), stop=(dc == DC - 1)),
                        r=[sq, ONESC], w=[PS[7]], name=f"{tag}s2")
                ln_rows_and_bcast(512)
                for dc in range(DC):
                    tmp = SB5[dc % 2]
                    cmp("dve", lambda e, dc=dc, sl=sl, tmp=tmp: e.tensor_tensor(
                        tmp[:, 0:512], src(dc)[:, sl], PS[4][:, 0:512], ALU.mult),
                        r=[srcT(dc), PS[4]], w=[tmp], name=f"{tag}n1")
                    if not gb:
                        cmp("dve", lambda e, dc=dc, sl=sl, tmp=tmp: e.tensor_sub(
                            XN[dc][:, sl], tmp[:, 0:512], PS[5][:, 0:512]),
                            r=[tmp, PS[5]], w=[XN[dc]], name=f"{tag}n2")
                    else:
                        gams, bets = gb
                        cmp("dve", lambda e, dc=dc, sl=sl, tmp=tmp: e.tensor_sub(
                            tmp[:, 0:512], tmp[:, 0:512], PS[5][:, 0:512]),
                            r=[tmp, PS[5]], w=[tmp], name=f"{tag}n2")
                        cmp("dve", lambda e, dc=dc, sl=sl, tmp=tmp, gams=gams, bets=bets: e.tensor_scalar(
                            XN[dc][:, sl], tmp[:, 0:512],
                            gams[:, dc:dc + 1], bets[:, dc:dc + 1],
                            ALU.mult, ALU.add),
                            r=[tmp, gams, bets], w=[XN[dc]], name=f"{tag}n3")

    def ln_rows_and_bcast(w):
        cmp("dve", lambda e, w=w: e.tensor_scalar_mul(
            rowslc(0, w), PS[6][0:1, 0:w], 1.0 / D), r=[PS[6]], w=[ROWS], small=True)
        cmp("dve", lambda e, w=w: e.tensor_scalar_mul(
            rowslc(1, w), PS[7][0:1, 0:w], 1.0 / D), r=[PS[7]], w=[ROWS], small=True)
        cmp("dve", lambda e, w=w: e.tensor_tensor(
            rowslc(2, w), rowslc(0, w), rowslc(0, w), ALU.mult), r=[ROWS], w=[ROWS], small=True)
        cmp("dve", lambda e, w=w: e.tensor_sub(
            rowslc(1, w), rowslc(1, w), rowslc(2, w)), r=[ROWS], w=[ROWS], small=True)
        cmp("act", lambda e, w=w: e.activation(
            out=rowslc(2, w), in_=rowslc(1, w), func=AF.Ln, bias=LN_EPS),
            r=[ROWS, CONSTS], w=[ROWS], small=True)
        cmp("act", lambda e, w=w: e.activation(
            out=rowslc(2, w), in_=rowslc(2, w), func=AF.Exp, scale=-0.5),
            r=[ROWS], w=[ROWS], small=True)
        cmp("dve", lambda e, w=w: e.tensor_tensor(
            rowslc(1, w), rowslc(0, w), rowslc(2, w), ALU.mult), r=[ROWS], w=[ROWS], small=True)
        cmp("pe", lambda e, w=w: e.matmul(
            PS[4][:, 0:w], ONESR[0:1, :], rowslc(2, w), start=True, stop=True),
            r=[ROWS, ONESR], w=[PS[4]])
        cmp("pe", lambda e, w=w: e.matmul(
            PS[5][:, 0:w], ONESR[0:1, :], rowslc(1, w), start=True, stop=True),
            r=[ROWS, ONESR], w=[PS[5]])

    def phase1(dst, is_out):
        st = contextlib.ExitStack()
        sb = lambda nm, shp, dt=F32: st.enter_context(nc.sbuf_tensor(nm, shp, dt))
        WI = [sb(f"p1wi{i}", [128, 8 * 128], BF16) for i in range(2)]
        XBB = [sb(f"p1xb{i}", [128, 512], BF16) for i in range(2)]
        SQB = [sb(f"p1sq{i}", [128, 512], BF16) for i in range(2)]
        XC = sb("p1xc", [128, TPC + 2])
        SCA = [sb("p1a0", [128, TPC])]
        SCB = sb("p1bx", [128, TPC])
        SCS = sb("p1s", [128, TPC])
        SSUM = sb("p1ss", [128, TPC])
        DX = sb("p1dx", [128, TPC])
        DT = sb("p1dt", [128, TPC])
        XDBL = sb("p1xd", [96, TPC])
        XDBC = sb("p1xdc", [80, TPC], BF16)
        BPB = sb("p1bpbuf", [NST, TPC], BF16)
        WO = [sb(f"p1wo{i}", [128, 1024], BF16) for i in range(2)]
        SELS = sb("sels", [NST, NST * 128], BF16)
        dma("pool", SELS[:, :], selI[:, :], w=[SELS], sem_key=("sp1", "d"))
        CONVS = sb("convs", [128, 3 * DC])
        BDTS = sb("bdts", [128, DC])
        NLAM = sb("nlam", [128, NST * DC])
        DPRS = sb("dprs", [128, DC])
        WXTS = sb("wxts", [128, (R + 2 * NST) * DC], BF16)
        WDTS = sb("wdts", [R, D], BF16)
        WCTS = sb("wcts", [80, D], BF16)
        CAR1 = [sb(f"car1_{i}", [128, NST]) for i in range(DC)]
        CAR2 = [sb(f"car2_{i}", [128, NST]) for i in range(DC)]
        for i in range(DC):
            car1_ms = cmp("dve", lambda e, i=i: e.memset(CAR1[i][:, :], 0.0), w=[CAR1[i]])
            car1_ms.inc = True
            cmp("dve", lambda e, i=i: e.memset(CAR2[i][:, :], 0.0), w=[CAR2[i]])
        dma("sp", CONVS.ap().rearrange("p (a k) -> p a k", a=DC), convw.ap().rearrange("(a p) k -> p a k", p=128), w=[CONVS], sem_key=("su1", "d"))
        dma("sp", BDTS.ap().rearrange("p (a k) -> p a k", a=DC), bdt.ap().rearrange("(a p) k -> p a k", p=128), w=[BDTS], slow=True, sem_key=("su1", "d"))
        dma("sp", NLAM.ap().rearrange("p (a n) -> p a n", a=DC), Alog.ap().rearrange("(a p) n -> p a n", p=128), w=[NLAM], sem_key=("su1", "d"))
        dma("sp", DPRS.ap().rearrange("p (a k) -> p a k", a=DC), Dprm.ap().rearrange("(a p) k -> p a k", p=128), w=[DPRS], slow=True, sem_key=("su1", "d"))
        dma("pool", WXTS.ap().rearrange("p (a j) -> p a j", a=DC), WxT.ap().rearrange("(a p) j -> p a j", p=128), w=[WXTS], sem_key=("sp1", "d"))
        dma("pool", WDTS[:, :], WdtT[:, :], w=[WDTS], sem_key=("sp1", "d"))
        dma("pool", WCTS[64:80, :], WCT[:, :], w=[WCTS], sem_key=("sp1", "d"))
        cmp("act", lambda e: e.activation(out=NLAM[:, :], in_=NLAM[:, :], func=AF.Exp),
            r=[NLAM], w=[NLAM])
        cmp("dve", lambda e: e.tensor_scalar_mul(NLAM[:, :], NLAM[:, :], -1.0),
            r=[NLAM], w=[NLAM])
        dma("pool", WO16[:, :], WoutT[:, :], w=[WO16], name="wcvo")

        for tt in range(5):
            w = 512 if tt < 4 else 2
            for phase in ("stats", "norm"):
                for dc in range(DC):
                    xt = SB5[2 + dc % 2]
                    if tt < 4:
                        srcp = xw[dc * 128:(dc + 1) * 128, 1 + tt * 512:1 + (tt + 1) * 512]
                        dma("sp", xt[:, 0:512], srcp, w=[xt], name="p1lnld")
                    else:
                        dma("sp", xt[:, 0:1], xw[dc * 128:(dc + 1) * 128, 0:1],
                            w=[xt], name="p1lnlh0", slow=True)
                        dma("sp", xt[:, 1:2], xw[dc * 128:(dc + 1) * 128, TPC + 1:TPC + 2],
                            w=[xt], name="p1lnlh1", slow=True)
                    if phase == "stats":
                        xb, sq = XBB[dc % 2], SQB[dc % 2]
                        cmp("act", lambda e, w=w, xt=xt, xb=xb: e.activation(
                            out=xb[:, 0:w], in_=xt[:, 0:w], func=AF.Copy),
                            r=[xt], w=[xb])
                        cmp("act", lambda e, w=w, xt=xt, sq=sq: e.activation(
                            out=sq[:, 0:w], in_=xt[:, 0:w], func=AF.Square),
                            r=[xt], w=[sq])
                        cmp("pe", lambda e, dc=dc, w=w, xb=xb: e.matmul(
                            PS[6][0:1, 0:w], ONESC[:, 0:1], xb[:, 0:w],
                            start=(dc == 0), stop=(dc == DC - 1)),
                            r=[xb, ONESC], w=[PS[6]])
                        cmp("pe", lambda e, dc=dc, w=w, sq=sq: e.matmul(
                            PS[7][0:1, 0:w], ONESC[:, 0:1], sq[:, 0:w],
                            start=(dc == 0), stop=(dc == DC - 1)),
                            r=[sq, ONESC], w=[PS[7]])
                    else:
                        tmp = SB5[dc % 2]
                        dstt = XN[dc] if tt < 4 else XNH[dc]
                        dap = XN[dc][:, tt * 512:(tt + 1) * 512] if tt < 4 else XNH[dc][:, 0:2]
                        cmp("dve", lambda e, w=w, xt=xt, tmp=tmp: e.tensor_tensor(
                            tmp[:, 0:w], xt[:, 0:w], PS[4][:, 0:w], ALU.mult),
                            r=[xt, PS[4]], w=[tmp], small=(w == 2))
                        cmp("dve", lambda e, w=w, tmp=tmp, dap=dap, dstt=dstt: e.tensor_sub(
                            dap, tmp[:, 0:w], PS[5][:, 0:w]),
                            r=[tmp, PS[5]], w=[dstt], small=(w == 2))
                if phase == "stats":
                    if DEBUG_TAPS and tt == 4:
                        cmp("dve", lambda e: e.tensor_copy(ROWS[0:1, 4:6], PS[6][0:1, 0:2]),
                            r=[PS[6]], w=[ROWS])
                        cmp("dve", lambda e: e.tensor_copy(ROWS[0:1, 6:8], PS[7][0:1, 0:2]),
                            r=[PS[7]], w=[ROWS])
                    ln_rows_and_bcast(w)
                    if DEBUG_TAPS and tt == 4:
                        dma("sp", DBG["dbg_rows"][:, :], ROWS[:, :], r=[ROWS], name="dbgrows")

        if DEBUG_TAPS:
            for dc in range(DC):
                dma("pool", DBG["dbg_xn"][dc * 128:(dc + 1) * 128, :], XN[dc][:, :],
                    r=[XN[dc]], name="dbgxn")
                dma("pool", DBG["dbg_xnh"][dc * 128:(dc + 1) * 128, :], XNH[dc][:, :],
                    r=[XNH[dc]], name="dbgxnh")
        for ec in range(16):
            wi = WI[ec % 2]
            dma("pool", wi[:, :],
                WinT.ap().rearrange("(a p) ee -> p a ee", p=128)[:, :, ec * 128:(ec + 1) * 128],
                w=[wi], name="p1wild")
            for tt in range(4):
                psA = PS[tt % 2]
                for dc in range(DC):
                    cmp("pe", lambda e, tt=tt, dc=dc, wi=wi, psA=psA: e.matmul(
                        psA[:, 0:512], wi[:, dc * 128:(dc + 1) * 128],
                        XN[dc][:, tt * 512:(tt + 1) * 512],
                        start=(dc == 0), stop=(dc == DC - 1)),
                        r=[wi, XN[dc]], w=[psA], name="p1xz")
                if ec < 8:
                    cmp("dve", lambda e, tt=tt, psA=psA: e.tensor_copy(
                        XC[:, 1 + tt * 512:1 + (tt + 1) * 512], psA[:, 0:512]),
                        r=[psA], w=[XC], name="p1xcev")
                else:
                    sz = SB5[2 + tt % 2]
                    cmp("act", lambda e, sz=sz, psA=psA: e.activation(
                        out=sz[:, 0:512], in_=psA[:, 0:512], func=AF.Silu),
                        r=[psA], w=[sz], name="p1silz")
                    dma("sp", SILZD[(ec - 8) * 128:(ec - 7) * 128, tt * 512:(tt + 1) * 512],
                        sz[:, 0:512], r=[sz], w=[SILZD], name="p1szst")
            psB = PS[2]
            for dc in range(DC):
                cmp("pe", lambda e, dc=dc, wi=wi, psB=psB: e.matmul(
                    psB[:, 0:2], wi[:, dc * 128:(dc + 1) * 128], XNH[dc][:, 0:2],
                    start=(dc == 0), stop=(dc == DC - 1)),
                    r=[wi, XNH[dc]], w=[psB], name="p1xzh")
            if ec < 8:
                cmp("dve", lambda e, psB=psB: e.tensor_copy(
                    XC[:, 0:1], psB[:, 0:1]), r=[psB], w=[XC], small=True)
                cmp("dve", lambda e, psB=psB: e.tensor_copy(
                    XC[:, TPC + 1:TPC + 2], psB[:, 1:2]), r=[psB], w=[XC], small=True)
                w0 = CONVS[:, ec * 3 + 0:ec * 3 + 1]
                w1 = CONVS[:, ec * 3 + 1:ec * 3 + 2]
                w2 = CONVS[:, ec * 3 + 2:ec * 3 + 3]
                cmp("dve", lambda e, w1=w1: e.tensor_scalar(
                    DX[:, :], XC[:, 1:TPC + 1], w1, None, ALU.mult),
                    r=[XC, CONVS], w=[DX], name="p1cv1")
                cmp("dve", lambda e, w0=w0: e.scalar_tensor_tensor(
                    DX[:, :], XC[:, 0:TPC], w0, DX[:, :], ALU.mult, ALU.add),
                    r=[XC, CONVS, DX], w=[DX], name="p1cv2")
                cmp("dve", lambda e, w2=w2: e.scalar_tensor_tensor(
                    DX[:, :], XC[:, 2:TPC + 2], w2, DX[:, :], ALU.mult, ALU.add),
                    r=[XC, CONVS, DX], w=[DX], name="p1cv3")
                if DEBUG_TAPS and ec == 0:
                    dma("sp", DBG["dbg_xc"][:, :], XC[:, :], r=[XC], name="dbgxc")
                cmp("act", lambda e, ec=ec: e.activation(
                    out=XA[ec][:, :], in_=DX[:, :], func=AF.Silu),
                    r=[DX], w=[XA[ec]], name="p1silxs")

        if DEBUG_TAPS:
            for dc in range(DC):
                dma("sp", DBG["dbg_xs"][dc * 128:(dc + 1) * 128, :], XA[dc][:, :],
                    r=[XA[dc]], name="dbgxs")
        for tt in range(4):
            ps = PS[4 + tt % 2]
            for dc in range(DC):
                xb = XBB[dc % 2]
                cmp("dve", lambda e, tt=tt, dc=dc, xb=xb: e.tensor_copy(
                    xb[:, 0:512], XA[dc][:, tt * 512:(tt + 1) * 512]),
                    r=[XA[dc]], w=[xb], name="p1xsbf")
                cmp("pe", lambda e, tt=tt, dc=dc, ps=ps, xb=xb: e.matmul(
                    ps[0:96, 0:512], WXTS[:, dc * 96:(dc + 1) * 96], xb[:, 0:512],
                    start=(dc == 0), stop=(dc == DC - 1)),
                    r=[WXTS, xb], w=[ps], name="p1xdbl")
            cmp("dve", lambda e, tt=tt, ps=ps: e.tensor_copy(
                XDBL[:, tt * 512:(tt + 1) * 512], ps[0:96, 0:512]),
                r=[ps], w=[XDBL], name="p1xdev")

        dma("pool", XDBC[0:64, :], XDBL[32:96, :], r=[XDBL], w=[XDBC], name="p1dtb")
        dma("pool", XDBC[64:80, :], XDBL[16:32, :], r=[XDBL], w=[XDBC], name="p1cpb")
        dma("pool", BPB[:, :], XDBL[0:16, :], r=[XDBL], w=[BPB], name="p1bpb")

        if DEBUG_TAPS:
            dma("sp", DBG["dbg_xd"][:, :], XDBL[:, :], r=[XDBL], name="dbgxd")
        for dc in range(DC):
            for tt in range(4):
                ps = PS[4 + tt % 2]
                cmp("pe", lambda e, dc=dc, tt=tt, ps=ps: e.matmul(
                    ps[:, 0:512], WDTS[:, dc * 128:(dc + 1) * 128],
                    XDBC[0:64, tt * 512:(tt + 1) * 512], start=True, stop=True),
                    r=[WDTS, XDBC], w=[ps], name="p1dtmm")
                bd = BDTS[:, dc:dc + 1]
                ab, rl = SB5[0], SB5[1]
                cmp("act", lambda e, ps=ps, bd=bd, ab=ab: e.activation(
                    out=ab[:, 0:512], in_=ps[:, 0:512], func=AF.Abs, bias=bd),
                    r=[ps, BDTS], w=[ab], name="p1abs")
                cmp("act", lambda e, ps=ps, bd=bd, rl=rl: e.activation(
                    out=rl[:, 0:512], in_=ps[:, 0:512], func=AF.Relu, bias=bd),
                    r=[ps, BDTS], w=[rl], name="p1relu")
                cmp("act", lambda e, ab=ab: e.activation(
                    out=ab[:, 0:512], in_=ab[:, 0:512], func=AF.Exp, scale=-1.0),
                    r=[ab], w=[ab], name="p1exp")
                cmp("act", lambda e, ab=ab: e.activation(
                    out=ab[:, 0:512], in_=ab[:, 0:512], func=AF.Ln, bias=1.0),
                    r=[ab], w=[ab], name="p1ln1p")
                cmp("dve", lambda e, tt=tt, ab=ab, rl=rl: e.scalar_tensor_tensor(
                    DT[:, tt * 512:(tt + 1) * 512], ab[:, 0:512], CONSTS[:, 0:1],
                    rl[:, 0:512], ALU.add, ALU.add),
                    r=[ab, rl, CONSTS], w=[DT], name="p1dtc")
            cmp("dve", lambda e, dc=dc: e.tensor_mul(DX[:, :], DT[:, :], XA[dc][:, :]),
                r=[DT, XA[dc]], w=[DX], name="p1dxs")

            def carry_wait(e, prog, dc=dc):
                preg = get_parity(e, prog)
                t = get_reg(e, prog, "cwt")
                e.reg_mul(t, preg, 16)
                e.wait_ge(prog.sems[("carry", dc)], t)
            prog.raw("dve", carry_wait, name="p1cwait")

            for n in range(NST):
                sca_t = SCA[0] if n % 2 == 0 else XC
                sca = sca_t[:, 0:TPC]
                for tt in range(4):
                    ps = PS[tt % 2]
                    cmp("pe", lambda e, n=n, tt=tt, ps=ps: e.matmul(
                        ps[:, 0:512], SELS[:, n * 128:(n + 1) * 128],
                        BPB[:, tt * 512:(tt + 1) * 512], start=True, stop=True),
                        r=[SELS, BPB], w=[ps], name="p1bb")
                    cmp("dve", lambda e, tt=tt, ps=ps: e.tensor_mul(
                        SCB[:, tt * 512:(tt + 1) * 512], DX[:, tt * 512:(tt + 1) * 512],
                        ps[:, 0:512]),
                        r=[DX, ps], w=[SCB], name="p1bx")
                cmp("act", lambda e, dc=dc, n=n, sca=sca: e.activation(
                    out=sca, in_=DT[:, :], func=AF.Exp,
                    scale=NLAM[:, dc * NST + n:dc * NST + n + 1]),
                    r=[DT, NLAM], w=[sca_t], name="p1adisc")
                scop = prog.add("dve", lambda e, dc=dc, n=n, sca=sca: e.tensor_tensor_scan(
                    SCS[:, :], sca, SCB[:, :], CAR2[dc][:, n:n + 1],
                    ALU.mult, ALU.add),
                    r=[sca_t, SCB], w=[SCS], kind="cmp", name="p1scan")
                scop.inc = True
                if n == 0:
                    ssop = cmp("dve", lambda e: e.tensor_copy(SSUM[:, :], SCS[:, :]),
                               r=[SCS], w=[SSUM], name="p1ss0")
                else:
                    ssop = cmp("dve", lambda e: e.tensor_add(SSUM[:, :], SSUM[:, :], SCS[:, :]),
                               r=[SCS, SSUM], w=[SSUM], name="p1ssa")
                ssop.inc = True

                def car_copy(e, prog, dc=dc, n=n, scop=scop):
                    preg = get_parity(e, prog)
                    e.wait_ge(prog.sems[("dve", "c")], scop.num)
                    with e.If_eq(preg, 0):
                        e.tensor_copy(CAR1[dc][:, n:n + 1],
                                      SCS[:, TPC - 1:TPC]).then_inc(
                            prog.sems[("carc", "c")], 1)
                    with e.Else():
                        e.nop().then_inc(prog.sems[("carc", "c")], 1)
                    e.end_ifs()
                prog.raw("dve", car_copy, name="p1ccp")
            if DEBUG_TAPS and dc == 0:
                dma("sp", DBG["dbg_ss"][:, :], SSUM[:, :], r=[SSUM], name="dbgss")
                dma("sp", DBG["dbg_dt"][:, :], DT[:, :], r=[DT], name="dbgdt")
            def contrib_wait(e, prog, dc=dc, ms=car1_ms):
                preg = get_parity(e, prog)
                e.wait_ge(prog.sems[("dve", "c")], ms.num)
                t = get_reg(e, prog, "cwt2")
                e.reg_mov(t, 1)
                e.reg_sub(t, t, preg)
                e.reg_mul(t, t, NST * (dc + 1))
                e.wait_ge(prog.sems[("carc", "c")], t)
            prog.raw("pool", contrib_wait, name="p1cw2")
            dma("pool", CCCI[dc * 128:(dc + 1) * 128, :], CAR1[dc][:, :],
                w=[CCCI], name="p1ctx")
            cc = prog.add("pool", lambda e, dc=dc: e.collective_compute(
                "AllGather", ALU.bypass, replica_groups=RG,
                ins=[CCCI[dc * 128:(dc + 1) * 128, :]],
                outs=[CCCO[dc * 256:(dc + 1) * 256, :]]),
                r=[CCCI], w=[CCCO], kind="cc", name="p1cc")
            dma("pool", CAR2[dc][:, :], CCCO[dc * 256:dc * 256 + 128, :],
                r=[CCCO], w=[CAR2[dc]], name="p1crx", sem_key=("carry", dc))
            dma("pool", GW16[dc, :, :], gwT[dc, :, :], w=[GW16], name="wcvg")
            dma("pool", UW16[dc, :, :], uwT[dc, :, :], w=[UW16], name="wcvu")
            dma("pool", DW16[dc, :, :], dwT[dc, :, :], w=[DW16], name="wcvd")

            for tt in range(4):
                ps = PS[6]
                cmp("pe", lambda e, dc=dc, tt=tt, ps=ps: e.matmul(
                    ps[:, 0:512], WCTS[64:80, dc * 128:(dc + 1) * 128],
                    XDBC[64:80, tt * 512:(tt + 1) * 512], start=True, stop=True),
                    r=[WCTS, XDBC], w=[ps], name="p1cmm")
                t2 = SB5[2]
                cmp("dve", lambda e, tt=tt, ps=ps, t2=t2: e.tensor_mul(
                    t2[:, 0:512], SSUM[:, tt * 512:(tt + 1) * 512], ps[:, 0:512]),
                    r=[SSUM, ps], w=[t2], name="p1ymul")
                dp = DPRS[:, dc:dc + 1]
                cmp("dve", lambda e, dc=dc, tt=tt, t2=t2, dp=dp: e.scalar_tensor_tensor(
                    XN[dc][:, tt * 512:(tt + 1) * 512],
                    XA[dc][:, tt * 512:(tt + 1) * 512], dp, t2[:, 0:512],
                    ALU.mult, ALU.add),
                    r=[XA[dc], DPRS, t2], w=[XN[dc]], name="p1yc")

        if DEBUG_TAPS:
            for dc in range(DC):
                dma("pool", DBG["dbg_y"][dc * 128:(dc + 1) * 128, :], XN[dc][:, :],
                    r=[XN[dc]], name="dbgy")
        for oc in range(DC):
            wo = WO[oc % 2]
            dma("act", wo[:, :],
                WO16.ap().rearrange("(a p) ee -> p a ee", p=128)[:, :, oc * 128:(oc + 1) * 128],
                r=[WO16], w=[wo], name="p1wold")
            for tt in range(4):
                ps = PS[oc % 2]
                for dc in range(DC):
                    cmp("pe", lambda e, tt=tt, dc=dc, wo=wo, ps=ps: e.matmul(
                        ps[:, 0:512], wo[:, dc * 128:(dc + 1) * 128],
                        XN[dc][:, tt * 512:(tt + 1) * 512],
                        start=(dc == 0), stop=(dc == DC - 1)),
                        r=[wo, XN[dc]], w=[ps], name="p1omm")
                szt = SB5[2 + tt % 2]
                xt = SB5[tt % 2]
                dma("sp", szt[:, 0:512],
                    SILZD[oc * 128:(oc + 1) * 128, tt * 512:(tt + 1) * 512],
                    r=[SILZD], w=[szt], name="p1szld")
                dma("sp", xt[:, 0:512],
                    xw[oc * 128:(oc + 1) * 128, 1 + tt * 512:1 + (tt + 1) * 512],
                    w=[xt], name="p1xld")
                cmp("dve", lambda e, ps=ps, szt=szt: e.tensor_mul(
                    szt[:, 0:512], ps[:, 0:512], szt[:, 0:512]),
                    r=[ps, szt], w=[szt], name="p1oz")
                cmp("dve", lambda e, oc=oc, tt=tt, szt=szt, xt=xt: e.tensor_add(
                    XA[oc][:, tt * 512:(tt + 1) * 512], szt[:, 0:512], xt[:, 0:512]),
                    r=[szt, xt], w=[XA[oc]], name="p1res")
                if is_out:
                    dma("sp", dst[oc * 128:(oc + 1) * 128, tt * 512:(tt + 1) * 512],
                        XA[oc][:, tt * 512:(tt + 1) * 512],
                        r=[XA[oc]], w=[dst], name="p1xrst")
        st.close()

    def phase_moe(idx, dst, is_out):
        st = contextlib.ExitStack()
        sb = lambda nm, shp, dt=F32: st.enter_context(nc.sbuf_tensor(nm, shp, dt))
        HBT = [sb(f"m{idx}h{i}", [128, TPC // 2], BF16) for i in range(22)]
        GWB = [sb(f"m{idx}g{i}", [128, 8 * 128], BF16) for i in range(2)]
        UWB = [sb(f"m{idx}u{i}", [128, 8 * 128], BF16) for i in range(2)]
        DWB = [sb(f"m{idx}d{i}", [128, 11 * 128], BF16) for i in range(4)]
        WTE = sb(f"m{idx}wte", [128, 128])
        WTT = sb(f"m{idx}wtt", [8, TPC])
        WBC = sb(f"m{idx}wbc", [128, TPC])
        EYE = sb(f"m{idx}eye", [128, 128])
        dma("sp", EYE[:, :], eyeI[:, :], w=[EYE], sem_key=(f"su{idx}", "d"))
        SELS = sb(f"m{idx}sels", [8, 8 * 128])
        dma("sp", SELS[:, :], selI[0:8, 0:8 * 128], w=[SELS], sem_key=(f"su{idx}", "d"))
        RWS = sb(f"m{idx}rws", [128, E * DC], BF16)
        dma("pool", RWS.ap().rearrange("p (a j) -> p a j", a=DC), rwT.ap().rearrange("(a p) j -> p a j", p=128), w=[RWS], sem_key=(f"sp{idx}", "d"))
        if idx == 2:
            dma("pool", W116[:, :], W1[:, :], w=[W116], name="wcv1")
            dma("pool", W216[:, :], W2[:, :], w=[W216], name="wcv2")

        emit_ln(lambda dc: XA[dc][:, :], lambda dc: XA[dc], None, f"m{idx}l")

        for ci in range(16):
            ps = PS[0]
            for dc in range(DC):
                cmp("pe", lambda e, ci=ci, dc=dc, ps=ps: e.matmul(
                    ps[:, 0:E], XN[dc][:, ci * 128:(ci + 1) * 128],
                    RWS[:, dc * E:(dc + 1) * E],
                    start=(dc == 0), stop=(dc == DC - 1)),
                    r=[XN[dc], RWS], w=[ps], name=f"m{idx}rt")
            t0, t1 = SB5[0], SB5[1]
            cmp("dve", lambda e, ps=ps, t0=t0: e.reduce_max(
                t0[:, 0:1], ps[:, 0:E], axis=AX.X, negate=True), r=[ps], w=[t0], small=True)
            cmp("act", lambda e, ps=ps, t0=t0, t1=t1: e.activation(
                out=t1[:, 0:E], in_=ps[:, 0:E], func=AF.Exp, bias=t0[:, 0:1]),
                r=[ps, t0], w=[t1], small=True)
            cmp("dve", lambda e, t1=t1, t0=t0: e.reduce_sum(
                t0[:, 1:2], t1[:, 0:E], axis=AX.X), r=[t1], w=[t0], small=True)
            cmp("dve", lambda e, t0=t0: e.reciprocal(t0[:, 2:3], t0[:, 1:2]),
                r=[t0], w=[t0], small=True)
            cmp("dve", lambda e, t1=t1, t0=t0: e.tensor_scalar(
                t1[:, 8:16], t1[:, 0:E], t0[:, 2:3], None, ALU.mult),
                r=[t1, t0], w=[t1], small=True)
            cmp("dve", lambda e, t1=t1, t0=t0: e.reduce_max(
                t0[:, 3:4], t1[:, 8:16], axis=AX.X), r=[t1], w=[t0], small=True)
            cmp("dve", lambda e, t1=t1, t0=t0: e.tensor_scalar(
                t1[:, 16:24], t1[:, 8:16], t0[:, 3:4], -1e30, ALU.is_ge, ALU.mult),
                r=[t1, t0], w=[t1], small=True)
            cmp("dve", lambda e, t1=t1: e.tensor_add(
                t1[:, 16:24], t1[:, 16:24], t1[:, 8:16]), r=[t1], w=[t1], small=True)
            cmp("dve", lambda e, t1=t1, t0=t0: e.reduce_max(
                t0[:, 4:5], t1[:, 16:24], axis=AX.X), r=[t1], w=[t0], small=True)
            cmp("dve", lambda e, t1=t1, t0=t0: e.tensor_scalar(
                t1[:, 24:32], t1[:, 8:16], t0[:, 4:5], None, ALU.is_ge),
                r=[t1, t0], w=[t1], small=True)
            cmp("dve", lambda e, ci=ci, t1=t1: e.tensor_mul(
                WTE[:, ci * 8:(ci + 1) * 8], t1[:, 24:32], t1[:, 8:16]),
                r=[t1], w=[WTE], name=f"m{idx}wte", small=True)
        for ci in range(16):
            ps = PS[1]
            cmp("pe", lambda e, ci=ci, ps=ps: e.transpose(
                ps[0:8, 0:128], WTE[:, ci * 8:(ci + 1) * 8], EYE[:, :]),
                r=[WTE, EYE], w=[ps], name=f"m{idx}tr")
            cmp("dve", lambda e, ci=ci, ps=ps: e.tensor_copy(
                WTT[:, ci * 128:(ci + 1) * 128], ps[0:8, 0:128]),
                r=[ps], w=[WTT], name=f"m{idx}trc", small=True)

        for ex in range(E):
            for tt in range(4):
                ps = PS[6]
                cmp("pe", lambda e, ex=ex, tt=tt, ps=ps: e.matmul(
                    ps[:, 0:512], SELS[0:8, ex * 128:(ex + 1) * 128],
                    WTT[:, tt * 512:(tt + 1) * 512], start=True, stop=True),
                    r=[WTT, SELS], w=[ps], name=f"m{idx}wb")
                cmp("dve", lambda e, tt=tt, ps=ps: e.tensor_copy(
                    WBC[:, tt * 512:(tt + 1) * 512], ps[:, 0:512]),
                    r=[ps], w=[WBC], name=f"m{idx}wbc")
            def issue_dld(ex, oc_):
                for ih in range(2):
                    dw = DWB[(oc_ % 2) * 2 + ih]
                    dma("act", dw[:, :],
                        DW16[ex, ih * 11 * 128:(ih + 1) * 11 * 128, :].rearrange(
                            "(a p) h -> p a h", p=128)[:, :, oc_ * 128:(oc_ + 1) * 128],
                        r=[DW16], w=[dw], name=f"m{idx}dld")
            for th in range(2):
                issue_dld(ex, 0)
                for ic in range(22):
                    gw, uw = GWB[ic % 2], UWB[ic % 2]
                    dma("pool", gw[:, :],
                        GW16[ex, :, :].rearrange("(a p) i -> p a i", p=128)[:, :, ic * 128:(ic + 1) * 128],
                        r=[GW16], w=[gw], name=f"m{idx}gld")
                    dma("pool", uw[:, :],
                        UW16[ex, :, :].rearrange("(a p) i -> p a i", p=128)[:, :, ic * 128:(ic + 1) * 128],
                        r=[UW16], w=[uw], name=f"m{idx}uld")
                    for tt2 in range(2):
                        tt = th * 2 + tt2
                        psg, psu = PS[tt2], PS[2 + tt2]
                        for dc in range(DC):
                            cmp("pe", lambda e, gw=gw, tt=tt, dc=dc, psg=psg: e.matmul(
                                psg[:, 0:512],
                                gw[:, dc * 128:(dc + 1) * 128],
                                XN[dc][:, tt * 512:(tt + 1) * 512],
                                start=(dc == 0), stop=(dc == DC - 1)),
                                r=[gw, XN[dc]], w=[psg], name=f"m{idx}gmm")
                        for dc in range(DC):
                            cmp("pe", lambda e, uw=uw, tt=tt, dc=dc, psu=psu: e.matmul(
                                psu[:, 0:512],
                                uw[:, dc * 128:(dc + 1) * 128],
                                XN[dc][:, tt * 512:(tt + 1) * 512],
                                start=(dc == 0), stop=(dc == DC - 1)),
                                r=[uw, XN[dc]], w=[psu], name=f"m{idx}umm")
                        sg, t2 = SB5[tt2], SB5[2 + tt2]
                        cmp("act", lambda e, psg=psg, sg=sg: e.activation(
                            out=sg[:, 0:512], in_=psg[:, 0:512], func=AF.Silu),
                            r=[psg], w=[sg], name=f"m{idx}sig")
                        cmp("dve", lambda e, tt=tt, psu=psu, t2=t2: e.tensor_mul(
                            t2[:, 0:512], psu[:, 0:512], WBC[:, tt * 512:(tt + 1) * 512]),
                            r=[psu, WBC], w=[t2], name=f"m{idx}uw")
                        cmp("dve", lambda e, ic=ic, tt2=tt2, sg=sg, t2=t2: e.tensor_mul(
                            HBT[ic][:, tt2 * 512:(tt2 + 1) * 512], t2[:, 0:512], sg[:, 0:512]),
                            r=[t2, sg], w=[HBT[ic]], name=f"m{idx}hm")
                for oc in range(DC):
                    if oc + 1 < DC:
                        issue_dld(ex, oc + 1)
                    for tt2 in range(2):
                        tt = th * 2 + tt2
                        ps = PS[4 + (oc * 2 + tt2) % 2]
                        for ih in range(2):
                            dw = DWB[(oc % 2) * 2 + ih]
                            for ic2 in range(11):
                                ic = ih * 11 + ic2
                                cmp("pe", lambda e, dw=dw, ic=ic, ic2=ic2, tt2=tt2, ps=ps: e.matmul(
                                    ps[:, 0:512], dw[:, ic2 * 128:(ic2 + 1) * 128],
                                    HBT[ic][:, tt2 * 512:(tt2 + 1) * 512],
                                    start=(ic == 0), stop=(ic == 21)),
                                    r=[dw, HBT[ic]], w=[ps], name=f"m{idx}dmm")
                        cmp("dve", lambda e, oc=oc, tt=tt, ps=ps: e.tensor_add(
                            XA[oc][:, tt * 512:(tt + 1) * 512],
                            XA[oc][:, tt * 512:(tt + 1) * 512], ps[:, 0:512]),
                            r=[ps, XA[oc]], w=[XA[oc]], name=f"m{idx}ya")
        for dc in range(DC):
            dma("sp", dst[dc * 128:(dc + 1) * 128, :], XA[dc][:, :],
                r=[XA[dc]], w=[dst], name=f"m{idx}st")
        st.close()

    def phase3(dst, is_out):
        st = contextlib.ExitStack()
        sb = lambda nm, shp, dt=F32: st.enter_context(nc.sbuf_tensor(nm, shp, dt))
        XNP = [sb(f"a3xnp{i}", [128, TPC], BF16) for i in range(DC)]
        W1Q = sb("a3w1q", [128, 8 * 128], BF16)
        W1K = sb("a3w1k", [128, 8 * 128], BF16)
        W1V = sb("a3w1v", [128, 8 * 128], BF16)
        WUB = [sb(f"a3wu{i}", [128, 8 * 256], BF16) for i in range(2)]
        KH = sb("a3kh", [128, 2 * TPC], BF16)
        VH = [sb(f"a3vh{i}", [128, TPC], BF16) for i in range(2)]
        QHP = [sb(f"a3qhp{h}", [128, TPC], BF16) for h in range(2)]
        cmp("dve", lambda e: e.memset(QHP[0][64:128, :], 0.0), w=[QHP[0]])
        cmp("dve", lambda e: e.memset(QHP[1][0:64, :], 0.0), w=[QHP[1]])
        AT = [sb(f"a3at{i}", [128, TPC], BF16) for i in range(2)]
        UT = sb("a3ut", [128, TPC])
        B1S = sb("b1s", [128, 32])
        B1VB = sb("b1vb", [1, D], BF16)
        B2S = sb("b2s", [128, DC])
        GAMS = sb("gams", [128, DC])
        BETS = sb("bets", [128, DC])
        dma("sp", B1S.ap().rearrange("p (a k) -> p a k", a=32), b1.ap().rearrange("(a p) k -> p a k", p=128), w=[B1S], slow=True, sem_key=("su3", "d"))
        dma("pool", B1VB[:, :], b1[1024:2048, 0:1].rearrange("a k -> k a"), w=[B1VB], sem_key=("sp3", "d"))
        dma("sp", B2S.ap().rearrange("p (a k) -> p a k", a=DC), b2.ap().rearrange("(a p) k -> p a k", p=128), w=[B2S], slow=True, sem_key=("su3", "d"))
        dma("sp", GAMS.ap().rearrange("p (a k) -> p a k", a=DC), gamma.ap().rearrange("(a p) k -> p a k", p=128), w=[GAMS], slow=True, sem_key=("su3", "d"))
        dma("sp", BETS.ap().rearrange("p (a k) -> p a k", a=DC), beta.ap().rearrange("(a p) k -> p a k", p=128), w=[BETS], slow=True, sem_key=("su3", "d"))

        emit_ln(lambda dc: XA[dc][:, :], lambda dc: XA[dc], None, "a3l")

        for dc in range(DC):
            dma("pool", CCXI[dc * 128:(dc + 1) * 128, :], XN[dc][:, :],
                r=[XN[dc]], w=[CCXI], name="a3ctx")
        ccx = prog.add("pool", lambda e: e.collective_compute(
            "AllGather", ALU.bypass, replica_groups=RG,
            ins=[CCXI.ap()], outs=[CCXO.ap()]),
            r=[CCXI], w=[CCXO], kind="cc", name="a3cc")

        def xnp_load(e, prog):
            preg = get_parity(e, prog)
            t = get_reg(e, prog, "xoff")
            e.wait_ge(prog.sems[ccx.sem_key], ccx.num)
            e.reg_mov(t, 1)
            e.reg_sub(t, t, preg)
            e.reg_mul(t, t, 1024 * TPC)
            t2 = get_reg(e, prog, "xoff2")
            for dc in range(DC):
                e.reg_add(t2, t, dc * 128 * TPC)
                e.dma_start(out=XNP[dc][:, :],
                            in_=bass.AP(CCXO, t2, [[TPC, 128], [1, TPC]])).then_inc(
                    prog.sems[("xnp", "d")], 16)
        prog.raw("sp", xnp_load, name="a3xnpld")
        prog.sem_keys.add(("xnp", "d"))

        for uq in range(4):
            wub = WUB[uq % 2]
            dma("pool", wub[:, :],
                W116.ap().rearrange("(a p) ee -> p a ee", p=128)[:, :, uq * 256:(uq + 1) * 256],
                r=[W116], w=[wub], name="a3wuld")
            for uc2 in range(2):
                uc = uq * 2 + uc2
                for tt in range(4):
                    ps = PS[uc % 2]
                    for dc in range(DC):
                        cmp("pe", lambda e, uc2=uc2, tt=tt, dc=dc, ps=ps, wub=wub: e.matmul(
                            ps[:, 0:512], wub[:, dc * 256 + uc2 * 128:dc * 256 + (uc2 + 1) * 128],
                            XN[dc][:, tt * 512:(tt + 1) * 512],
                            start=(dc == 0), stop=(dc == DC - 1)),
                            r=[wub, XN[dc]], w=[ps], name="a3umm")
                    ub = SB5[2 + tt % 2]
                    cmp("act", lambda e, uc=uc, ps=ps, ub=ub: e.activation(
                        out=ub[:, 0:512], in_=ps[:, 0:512], func=AF.Silu,
                        bias=B1S[:, uc:uc + 1]),
                        r=[ps, B1S], w=[ub], name="a3usil")
                    dma("sp", UD[uc * 128:(uc + 1) * 128, tt * 512:(tt + 1) * 512],
                        ub[:, 0:512], r=[ub], w=[UD], name="a3ust")

        def xnp_wait(e, prog):
            e.wait_ge(prog.sems[("xnp", "d")], 16 * DC)

        for hp in range(8):
            qc0, kc0, vc0 = 2048 + hp * 128, 3072 + hp * 128, 1024 + hp * 128
            w1r = W116.ap().rearrange("(a p) ee -> p a ee", p=128)
            dma("pool", W1Q.ap().rearrange("p (a j) -> p a j", j=128),
                w1r[:, :, qc0:qc0 + 128], r=[W116], w=[W1Q], name="a3w1q")
            dma("pool", W1K.ap().rearrange("p (a j) -> p a j", j=128),
                w1r[:, :, kc0:kc0 + 128], r=[W116], w=[W1K], name="a3w1k")
            dma("pool", W1V.ap().rearrange("p (a j) -> p a j", j=128),
                w1r[:, :, vc0:vc0 + 128], r=[W116], w=[W1V], name="a3w1v")

            for tt in range(4):
                ps = PS[tt % 2]
                for dc in range(DC):
                    cmp("pe", lambda e, tt=tt, dc=dc, ps=ps: e.matmul(
                        ps[:, 0:512], W1Q[:, dc * 128:(dc + 1) * 128],
                        XN[dc][:, tt * 512:(tt + 1) * 512],
                        start=(dc == 0), stop=(dc == DC - 1)),
                        r=[W1Q, XN[dc]], w=[ps], name="a3qmm")
                for h2 in range(2):
                    pb = 64 * h2
                    cmp("act", lambda e, hp=hp, tt=tt, ps=ps, pb=pb, h2=h2: e.activation(
                        out=QHP[h2][pb:pb + 64, tt * 512:(tt + 1) * 512],
                        in_=ps[pb:pb + 64, 0:512], func=AF.Silu,
                        bias=B1S[pb:pb + 64, 16 + hp:16 + hp + 1]),
                        r=[ps, B1S], w=[QHP[h2]], name="a3qsil")
            if hp == 0:
                prog.raw("pe", xnp_wait, name="a3xnpw")
            for half in range(2):
                XSRC = XN if half == 0 else XNP
                for tt in range(4):
                    ps = PS[2 + tt % 2]
                    for dc in range(DC):
                        cmp("pe", lambda e, tt=tt, dc=dc, ps=ps, XSRC=XSRC: e.matmul(
                            ps[:, 0:512], W1K[:, dc * 128:(dc + 1) * 128],
                            XSRC[dc][:, tt * 512:(tt + 1) * 512],
                            start=(dc == 0), stop=(dc == DC - 1)),
                            r=[W1K, XSRC[dc]], w=[ps], name="a3kmm")
                    cmp("act", lambda e, hp=hp, half=half, tt=tt, ps=ps: e.activation(
                        out=KH[:, half * TPC + tt * 512:half * TPC + (tt + 1) * 512],
                        in_=ps[:, 0:512], func=AF.Silu,
                        bias=B1S[:, 24 + hp:24 + hp + 1]),
                        r=[ps, B1S], w=[KH], name="a3ksil")
            for half in range(2):
                XSRC = XN if half == 0 else XNP
                vh = VH[half]
                for kc in range(16):
                    ps = PS[2 + kc % 2]
                    for dc in range(DC):
                        cmp("pe", lambda e, kc=kc, dc=dc, ps=ps, XSRC=XSRC: e.matmul(
                            ps[:, 0:128], XSRC[dc][:, kc * 128:(kc + 1) * 128],
                            W1V[:, dc * 128:(dc + 1) * 128],
                            start=(dc == 0), stop=False),
                            r=[W1V, XSRC[dc]], w=[ps], name="a3vmm")
                    cmp("pe", lambda e, hp=hp, ps=ps: e.matmul(
                        ps[:, 0:128], ONESRB[0:1, :], B1VB[0:1, hp * 128:(hp + 1) * 128],
                        start=False, stop=True),
                        r=[ONESRB, B1VB], w=[ps], name="a3vb")
                    cmp("act", lambda e, kc=kc, vh=vh, ps=ps: e.activation(
                        out=vh[:, kc * 128:(kc + 1) * 128], in_=ps[:, 0:128], func=AF.Silu),
                        r=[ps], w=[vh], name="a3vsil")
            dma("sp", UT[:, :], UD[hp * 128:(hp + 1) * 128, :], r=[UD], w=[UT],
                name="a3uld")
            for h2 in range(2):
                pb = 64 * h2
                for kc in range(33):
                    if kc < 32:
                        at = AT[kc % 2]
                        for qt in range(4):
                            ps = PS[(kc * 4 + qt) % 2]
                            cmp("pe", lambda e, kc=kc, qt=qt, h2=h2, ps=ps: e.matmul(
                                ps[:, 0:512], KH[:, kc * 128:(kc + 1) * 128],
                                QHP[h2][:, qt * 512:(qt + 1) * 512],
                                start=True, stop=True),
                                r=[KH, QHP[h2]], w=[ps], name="a3smm")
                            cmp("act", lambda e, qt=qt, at=at, ps=ps: e.activation(
                                out=at[:, qt * 512:(qt + 1) * 512], in_=ps[:, 0:512],
                                func=AF.Silu),
                                r=[ps], w=[at], name="a3ssil")
                    if kc >= 1:
                        ka = kc - 1
                        at = AT[ka % 2]
                        half, kci = ka // 16, ka % 16
                        vh = VH[half]
                        for qt in range(4):
                            psv = PS[4 + qt]
                            vsl = vh[:, kci * 128:(kci + 1) * 128].rearrange(
                                "p (a j) -> p a j", j=64)[:, h2, :]
                            cmp("pe", lambda e, ka=ka, qt=qt, pb=pb, at=at, vsl=vsl, psv=psv: e.matmul(
                                psv[pb:pb + 64, 0:512], vsl, at[:, qt * 512:(qt + 1) * 512],
                                start=(ka == 0), stop=(ka == 31)),
                                r=[vh, at], w=[psv], name="a3avmm")
                for qt in range(4):
                    psv = PS[4 + qt]
                    cmp("dve", lambda e, hp=hp, h2=h2, qt=qt, pb=pb, psv=psv: e.scalar_tensor_tensor(
                        XA[hp][pb:pb + 64, qt * 512:(qt + 1) * 512],
                        psv[pb:pb + 64, 0:512], CONSTS[pb:pb + 64, 1:2],
                        UT[pb:pb + 64, qt * 512:(qt + 1) * 512],
                        ALU.mult, ALU.mult),
                        r=[psv, UT, CONSTS], w=[XA[hp]], name="a3avev")

        emit_ln(lambda dc: XA[dc][:, :], lambda dc: XA[dc], (GAMS, BETS), "a4l")
        for oq in range(4):
            wub = WUB[oq % 2]
            dma("pool", wub[:, :],
                W216.ap().rearrange("(a p) ee -> p a ee", p=128)[:, :, oq * 256:(oq + 1) * 256],
                r=[W216], w=[wub], name="a3w2ld")
            for oc2 in range(2):
                oc = oq * 2 + oc2
                for tt in range(4):
                    ps = PS[oc % 2]
                    for dc in range(DC):
                        cmp("pe", lambda e, oc2=oc2, tt=tt, dc=dc, ps=ps, wub=wub: e.matmul(
                            ps[:, 0:512], wub[:, dc * 256 + oc2 * 128:dc * 256 + (oc2 + 1) * 128],
                            XN[dc][:, tt * 512:(tt + 1) * 512],
                            start=(dc == 0), stop=(dc == DC - 1)),
                            r=[wub, XN[dc]], w=[ps], name="a3w2mm")
                    xt = SB5[tt % 2]
                    dma("sp", xt[:, 0:512],
                        XR[oc * 128:(oc + 1) * 128, tt * 512:(tt + 1) * 512],
                        r=[XR], w=[xt], name="a3rld")
                    cmp("dve", lambda e, oc=oc, tt=tt, ps=ps, xt=xt: e.scalar_tensor_tensor(
                        XA[oc][:, tt * 512:(tt + 1) * 512], ps[:, 0:512],
                        B2S[:, oc:oc + 1], xt[:, 0:512],
                        ALU.add, ALU.add),
                        r=[ps, B2S, xt], w=[XA[oc]], name="a3o")
                    if is_out:
                        dma("sp", dst[oc * 128:(oc + 1) * 128, tt * 512:(tt + 1) * 512],
                            XA[oc][:, tt * 512:(tt + 1) * 512],
                            r=[XA[oc]], w=[dst], name="a3st")
        st.close()

    dsts = [XR, XR, XR, out]
    dsts[n_sub - 1] = out
    phase1(dsts[0], n_sub == 1)
    if n_sub >= 2:
        prog.barrier()
        phase_moe(2, dsts[1], n_sub == 2)
    if n_sub >= 3:
        prog.barrier()
        phase3(dsts[2], n_sub == 3)
    if n_sub >= 4:
        prog.barrier()
        phase_moe(4, dsts[3], n_sub == 4)

    prog.raw("sp", lambda e, p: e.wait_ge(p.sems[("b", id(out))],
                                          16 * p.dma_counts[("b", id(out))]),
             name="finalwait")

    prog.finalize_numbering()
    keys = set(prog.sem_keys) | {("xnp", "d"), ("bar", "c"), ("carc", "c")}
    stack = contextlib.ExitStack()
    sems = {k: stack.enter_context(nc.semaphore(f"s_{k[0]}_{k[1]}")) for k in keys}
    with nc.Block() as block:
        prog.emit(block, sems)
    stack.close()
    return nc


def _host_prep(inputs):
    x = np.asarray(inputs["L0_attn_a_x"], dtype=np.float32)
    C = np.ascontiguousarray
    WinT = C(np.asarray(inputs["L0_attn_a_W_in"], np.float32).T)
    convw = C(np.asarray(inputs["L0_attn_a_conv_w"], np.float32))
    WxT0 = np.asarray(inputs["L0_attn_a_W_x"], np.float32).T
    perm = list(range(R, R + NST)) + list(range(R + NST, R + 2 * NST)) + list(range(R))
    WxT = C(WxT0[:, perm])
    WdtT = C(np.asarray(inputs["L0_attn_a_W_dt"], np.float32).T)
    bdt = C(np.asarray(inputs["L0_attn_a_b_dt"], np.float32).reshape(D, 1))
    Alog = C(np.asarray(inputs["L0_attn_a_A_log"], np.float32))
    Dprm = C(np.asarray(inputs["L0_attn_a_D_param"], np.float32).reshape(D, 1))
    WCT = C(np.asarray(inputs["L0_attn_a_W_C"], np.float32).T)
    WoutT = C(np.asarray(inputs["L0_attn_a_W_out"], np.float32).T)
    W1 = C(np.asarray(inputs["L0_attn_b_W1"], np.float32))
    b1 = C(np.asarray(inputs["L0_attn_b_b1"], np.float32).reshape(4 * D, 1))
    W2 = C(np.asarray(inputs["L0_attn_b_W2"], np.float32))
    b2 = C(np.asarray(inputs["L0_attn_b_b2"], np.float32).reshape(D, 1))
    gamma = C(np.asarray(inputs["L0_attn_b_gamma"], np.float32).reshape(D, 1))
    beta = C(np.asarray(inputs["L0_attn_b_beta"], np.float32).reshape(D, 1))
    rwT = C(np.asarray(inputs["L0_ffn_router_w"], np.float32).T)
    gwT = C(np.asarray(inputs["L0_ffn_gate_w"], np.float32).transpose(0, 2, 1))
    uwT = C(np.asarray(inputs["L0_ffn_up_w"], np.float32).transpose(0, 2, 1))
    dwT = C(np.asarray(inputs["L0_ffn_down_w"], np.float32).transpose(0, 2, 1))
    eye = np.eye(128, dtype=np.float32)
    sel = np.zeros((NST, NST * 128), np.float32)
    for n in range(NST):
        sel[n, n * 128:(n + 1) * 128] = 1.0

    in_maps = []
    for c in range(8):
        b, p = c // 2, c % 2
        xbT = C(x[b].T)
        xwin = np.zeros((D, TPC + 2), np.float32)
        t0 = p * TPC
        xwin[:, 1:TPC + 1] = xbT[:, t0:t0 + TPC]
        if t0 - 1 >= 0:
            xwin[:, 0] = xbT[:, t0 - 1]
        if t0 + TPC < T:
            xwin[:, TPC + 1] = xbT[:, t0 + TPC]
        cfgv = np.zeros((1, 8), np.int32)
        cfgv[0, 0] = p
        in_maps.append(dict(
            xw=xwin, WinT=WinT, convw=convw, WxT=WxT, WdtT=WdtT, bdt=bdt,
            Alog=Alog, Dprm=Dprm, WCT=WCT, WoutT=WoutT, W1=W1, b1=b1, W2=W2,
            b2=b2, gamma=gamma, beta=beta, rwT=rwT, gwT=gwT, uwT=uwT, dwT=dwT,
            cfg=cfgv, eyeI=eye, selI=sel))
    return in_maps


def kernel(**inputs):
    in_maps = _host_prep(inputs)
    nc = build_program(N_SUBBLOCKS)
    res = run_bass_kernel_spmd(nc, in_maps, core_ids=list(range(8)))
    outs = [res.results[c]["out"].T for c in range(8)]
    full = np.stack([np.concatenate([outs[2 * b], outs[2 * b + 1]], axis=0)
                     for b in range(B)], axis=0)
    return full.astype(np.float32)
